# version 05
# speedup vs baseline: 1.1905x; 1.0261x over previous
import math
from contextlib import ExitStack

import numpy as np
import ml_dtypes

import concourse.bass as bass
import concourse.mybir as mybir
from concourse.bass_utils import run_bass_kernel_spmd

F32 = mybir.dt.float32
BF16 = mybir.dt.bfloat16
AF = mybir.ActivationFunctionType
ALU = mybir.AluOpType
AX = mybir.AxisListType
BF16NP = ml_dtypes.bfloat16

FUSED = True
NCORES = 8
T = 4096
D = 1024

ENGS = ("sync", "tensor", "vector", "scalar", "gpsimd")



class K:

    DMA_POOL = 6

    def __init__(self, nc):
        self.nc = nc
        self.items = {e: [] for e in ENGS}
        self.cnt = {}
        self.sem_handles = {}
        self.nbar = 0
        for e in ENGS:
            self.cnt["p_" + e] = 0
            for i in range(self.DMA_POOL):
                self.cnt[f"dma_{e}{i}"] = 0
        self.cnt["bar"] = 0
        self._dma_rr = {e: 0 for e in ENGS}
        self._cst_vals = {}
        self._cst_buf = None
        self._cst_tok = None

    def init_cst_pool(self, st, n=64):
        self._cst_buf = st.enter_context(self.nc.sbuf_tensor("cstpool", [128, n], F32))

    def cst(self, val):
        val = float(val)
        if val not in self._cst_vals:
            idx = len(self._cst_vals)
            tok = self.step("gpsimd", lambda e, _i=idx, _v=val: e.memset(
                self._cst_buf[:, _i:_i + 1], _v))
            self._cst_vals[val] = (idx, tok)
        idx, _ = self._cst_vals[val]
        return self._cst_buf[:, idx:idx + 1]

    def ctok(self, val):
        return self._cst_vals[float(val)][1]

    @staticmethod
    def _flat(waits):
        out = []
        for w in waits:
            if w is None:
                continue
            if isinstance(w, list):
                out.extend(x for x in w if x is not None)
            else:
                out.append(w)
        best = {}
        for s, v in out:
            best[s] = max(best.get(s, 0), v)
        return tuple(best.items())

    def dma(self, q, out, in_, waits=(), **kw):
        i = self._dma_rr[q]
        self._dma_rr[q] = (i + 1) % self.DMA_POOL
        sem = f"dma_{q}{i}"
        prev = self.cnt[sem]
        self.cnt[sem] = prev + 16
        tok = (sem, prev + 16)
        waits = self._flat(waits)

        def fn(eng, _out=out, _in=in_, _w=waits, _kw=kw, _tok=tok, _prev=prev):
            for s, v in _w:
                eng.wait_ge(self.sem_handles[s], v)
            if _prev:
                eng.wait_ge(self.sem_handles[_tok[0]], _prev)
            eng.dma_start(out=_out, in_=_in, **_kw).then_inc(
                self.sem_handles[_tok[0]], 16)
        self.items[q].append(fn)
        return tok

    def step(self, e, fn, waits=(), inc=True):
        waits = self._flat(waits)
        tok = None
        if inc:
            self.cnt["p_" + e] += 1
            tok = ("p_" + e, self.cnt["p_" + e])

        def run(eng, _fn=fn, _w=waits, _tok=tok):
            for s, v in _w:
                eng.wait_ge(self.sem_handles[s], v)
            inst = _fn(eng)
            if _tok is not None:
                assert inst is not None, "step(inc=True) must return its instruction"
                inst.then_inc(self.sem_handles[_tok[0]], 1)
        self.items[e].append(run)
        return tok

    def barrier(self):
        self.nbar += 1
        target = 5 * self.nbar
        dma_targets = {e: [(f"dma_{e}{i}", self.cnt[f"dma_{e}{i}"])
                           for i in range(self.DMA_POOL) if self.cnt[f"dma_{e}{i}"]]
                       for e in ENGS}
        self.cnt["bar"] = target
        for e in ENGS:
            def fence(eng, _e=e, _dt=tuple(dma_targets[e]), _tg=target):
                for s, v in _dt:
                    eng.wait_ge(self.sem_handles[s], v)
                eng.nop().then_inc(self.sem_handles["bar"], 1)
                eng.wait_ge(self.sem_handles["bar"], _tg)
            self.items[e].append(fence)
        return ("bar", target)

    def emit(self):
        nc = self.nc
        with ExitStack() as st:
            for name in self.cnt:
                self.sem_handles[name] = st.enter_context(nc.semaphore(name))
            with nc.Block() as block:
                for e in ENGS:
                    items = self.items[e]
                    getattr(block, e)(lambda eng, _items=items: [f(eng) for f in _items])
        return nc




def build_consts():
    c = {}
    c["c_idn32"] = np.eye(128, dtype=np.float32)
    c["c_idnb"] = np.eye(128, dtype=BF16NP)
    sw = np.zeros((32, 32), np.float32)
    sw[16:32, 0:16] = np.eye(16)
    sw[0:16, 16:32] = np.eye(16)
    c["c_swap32"] = sw.astype(BF16NP)
    ROPE_BASE = 10000.0
    inv_freq = 1.0 / ROPE_BASE ** (np.arange(0, 32, 2, dtype=np.float64) / 32.0)
    fr = np.outer(np.arange(T, dtype=np.float64), inv_freq)
    cos = np.cos(fr).T.astype(np.float32)
    sin = np.sin(fr).T.astype(np.float32)
    cs = np.zeros((128, T), np.float32)
    ss = np.zeros((128, T), np.float32)
    for p0 in (0, 64):
        cs[p0:p0 + 16] = cos; cs[p0 + 16:p0 + 32] = cos
        ss[p0:p0 + 16] = sin; ss[p0 + 16:p0 + 32] = -sin
    c["c_cs32"] = cs
    c["c_ss32"] = ss
    rep2 = np.zeros((2, 128), np.float32)
    rep2[0, 0:64] = 1.0
    rep2[1, 64:128] = 1.0
    c["c_rep2"] = rep2
    c["c_triu64"] = np.triu(np.ones((64, 64), np.float32)).astype(BF16NP)
    stair = np.zeros((4 * 128, 512), np.float32)
    for j in range(4):
        for p in range(128):
            stair[j * 128 + p, :] = (np.arange(512) >= j * 128 + p)
    c["c_stair"] = stair.astype(BF16NP)
    c["c_ones128"] = np.ones((128, 128), BF16NP)
    sel = np.zeros((16, 16 * 128), np.float32)
    for n in range(16):
        sel[n, n * 128:(n + 1) * 128] = 1.0
    c["c_sel16"] = sel.astype(BF16NP)
    c["c_ones128f"] = np.ones((128, 128), np.float32)
    return c


CONST_SPECS = {
    "c_idn32": ([128, 128], F32),
    "c_idnb": ([128, 128], BF16),
    "c_swap32": ([32, 32], BF16),
    "c_cs32": ([128, T], F32),
    "c_ss32": ([128, T], F32),
    "c_rep2": ([2, 128], F32),
    "c_triu64": ([64, 64], BF16),
    "c_stair": ([512, 512], BF16),
    "c_ones128": ([128, 128], BF16),
    "c_sel16": ([16, 16 * 128], BF16),
    "c_ones128f": ([128, 128], F32),
}

WEIGHT_NAMES = [
    "L0_attn_Wq", "L0_attn_Wk", "L0_attn_Wv", "L0_attn_Wg", "L0_attn_Wgk1",
    "L0_attn_Wgk2", "L0_attn_bgk2", "L0_attn_norm_w", "L0_attn_Wo",
    "L0_ffn_w_router", "L0_ffn_b_router", "L0_ffn_w_experts", "L0_ffn_b_experts",
    "L1_attn_W_cq", "L1_attn_W_in", "L1_attn_q_gain", "L1_attn_conv_w",
    "L1_attn_conv_b", "L1_attn_W_xproj", "L1_attn_W_dt", "L1_attn_b_dt",
    "L1_attn_A_log", "L1_attn_D_param", "L1_attn_W_mout", "L1_attn_W_proj",
    "L1_attn_merge_alpha", "L1_ffn_router_w", "L1_ffn_w1", "L1_ffn_b1",
    "L1_ffn_w2", "L1_ffn_b2",
]

WEIGHT_SPECS = {
    "L0_attn_Wq": ([1024, 512], F32), "L0_attn_Wk": ([1024, 512], F32),
    "L0_attn_Wv": ([1024, 1024], F32), "L0_attn_Wg": ([1024, 1024], F32),
    "L0_attn_Wgk1": ([1024, 16], F32), "L0_attn_Wgk2": ([16, 512], F32),
    "L0_attn_bgk2": ([512], F32), "L0_attn_norm_w": ([128], F32),
    "L0_attn_Wo": ([1024, 1024], F32),
    "L0_ffn_w_router": ([8, 1024], F32), "L0_ffn_b_router": ([8], F32),
    "L0_ffn_w_experts": ([8, 1024, 1024], F32), "L0_ffn_b_experts": ([8, 1024], F32),
    "L1_attn_W_cq": ([1024, 1024], F32), "L1_attn_W_in": ([4608, 1024], F32),
    "L1_attn_q_gain": ([16], F32), "L1_attn_conv_w": ([2048, 4], F32),
    "L1_attn_conv_b": ([2048], F32), "L1_attn_W_xproj": ([96, 2048], F32),
    "L1_attn_W_dt": ([2048, 64], F32), "L1_attn_b_dt": ([2048], F32),
    "L1_attn_A_log": ([2048, 16], F32), "L1_attn_D_param": ([2048], F32),
    "L1_attn_W_mout": ([1024, 2048], F32), "L1_attn_W_proj": ([1024, 1024], F32),
    "L1_attn_merge_alpha": ([1], F32),
    "L1_ffn_router_w": ([8, 1024], F32), "L1_ffn_w1": ([8, 2048, 1024], F32),
    "L1_ffn_b1": ([8, 2048], F32), "L1_ffn_w2": ([8, 1024, 2048], F32),
    "L1_ffn_b2": ([8, 1024], F32),
}

SCRATCH_SPECS = {
    "xnT32": ([1024, T], F32),
    "maskT0": ([8, T], BF16),
    "g0T": ([1024, T], BF16),
    "out1": ([T, 1024], F32),
    "xn1T32": ([1024, T], F32),
    "combT1": ([8, T], BF16),
    "g1T": ([1024, T], BF16),
    "qhatT": ([1024, T], BF16),
    "khatT": ([256, T], BF16),
    "Vn": ([4, T, 64], BF16),
    "xssmT": ([2048, T], BF16),
    "gateT": ([2048, T], BF16),
    "attnT": ([1024, T], BF16),
    "wexp0_bf": ([8, 1024, 1024], BF16),
    "w1_bf": ([8, 2048, 1024], BF16),
    "w2_bf": ([8, 1024, 2048], BF16),
    "wcq_bf": ([1024, 1024], BF16),
    "win_bf": ([4608, 1024], BF16),
    "wmout_bf": ([1024, 2048], BF16),
    "wproj_bf": ([1024, 1024], BF16),
}

PHASE_IO = {
    "p0": (["x", "c_idn32"], ["xnT32"]),
    "p1": (["xnT32", "L0_ffn_w_router", "L0_ffn_b_router", "L0_ffn_w_experts",
            "L0_ffn_b_experts", "c_idn32", "c_idnb"],
           ["g0T", "maskT0", "wexp0_bf"]),
    "p2": (["x", "xnT32", "g0T", "L0_attn_Wq", "L0_attn_Wk", "L0_attn_Wv",
            "L0_attn_Wg", "L0_attn_Wgk1", "L0_attn_Wgk2", "L0_attn_bgk2",
            "L0_attn_norm_w", "L0_attn_Wo", "c_idn32", "c_idnb", "c_triu64",
            "c_ones128", "c_ones128f"], ["out1", "xn1T32"]),
    "p3": (["xn1T32", "L1_ffn_router_w", "L1_ffn_w1", "L1_ffn_b1", "L1_ffn_w2",
            "L1_ffn_b2", "c_idn32", "c_idnb", "c_ones128"],
           ["g1T", "combT1", "w1_bf", "w2_bf"]),
    "p4": (["xn1T32", "L1_attn_W_cq", "L1_attn_W_in", "L1_attn_q_gain",
            "c_swap32", "c_cs32", "c_ss32", "c_idnb", "c_ones128", "c_rep2"],
           ["qhatT", "khatT", "Vn", "xssmT", "gateT", "wcq_bf", "win_bf"]),
    "p5": (["qhatT", "khatT", "Vn", "L1_attn_q_gain", "c_stair", "c_ones128",
            "c_ones128f"], ["attnT"]),
    "p6": (["xssmT", "gateT", "attnT", "g1T", "out1", "L1_attn_conv_w",
            "L1_attn_conv_b", "L1_attn_W_xproj", "L1_attn_W_dt", "L1_attn_b_dt",
            "L1_attn_A_log", "L1_attn_D_param", "L1_attn_W_mout",
            "L1_attn_W_proj", "L1_attn_merge_alpha", "c_idnb", "c_idn32",
            "c_ones128", "c_ones128f", "c_sel16"],
           ["out", "wmout_bf", "wproj_bf"]),
}

PHASES = ["p0", "p1", "p2", "p3", "p4", "p5", "p6"]


def _specs():
    s = dict(CONST_SPECS)
    s.update(WEIGHT_SPECS)
    s["x"] = ([T, D], F32)
    s["out"] = ([T, D], F32)
    s.update(SCRATCH_SPECS)
    return s




def build_p0(k, io, st):
    nc = k.nc
    x, xnT = io["x"], io["xnT32"]
    xt = [st.enter_context(nc.sbuf_tensor(f"p0_xt{i}", [128, 1024], F32)) for i in range(2)]
    xc = [st.enter_context(nc.sbuf_tensor(f"p0_xc{i}", [128, 1024], F32)) for i in range(2)]
    dump = st.enter_context(nc.sbuf_tensor("p0_dump", [128, 1024], F32))
    s1 = st.enter_context(nc.sbuf_tensor("p0_s1", [128, 4], F32))
    ssq = st.enter_context(nc.sbuf_tensor("p0_ssq", [128, 1], F32))
    stg = [st.enter_context(nc.sbuf_tensor(f"p0_stg{i}", [128, 1024], F32)) for i in range(2)]
    idn = st.enter_context(nc.sbuf_tensor("p0_idn", [128, 128], F32))
    ps = [st.enter_context(nc.psum_tensor(f"p0_ps{i}", [128, 512], F32)) for i in range(2)]

    tok_idn = k.dma("sync", idn[:], io["c_idn32"])
    rel = {}
    for i in range(32):
        b = i % 2
        tl = k.dma("sync", xt[b][:], x[i * 128:(i + 1) * 128, :],
                   waits=[rel.get(("xt", b))])
        t_sum = k.step("vector", lambda e, _b=b: e.reduce_sum(
            s1[:, 0:1], xt[_b][:], axis=AX.X), waits=[tl])
        t_mean = k.step("vector", lambda e: e.tensor_scalar_mul(
            s1[:, 1:2], s1[:, 0:1], 1.0 / 1024.0), waits=[t_sum])
        t_xc = k.step("vector", lambda e, _b=b: e.tensor_scalar(
            xc[_b][:], xt[_b][:], s1[:, 1:2], None, ALU.subtract),
            waits=[t_mean, rel.get(("xc", b))])
        rel[("xt", b)] = t_xc
        t_sq = k.step("scalar", lambda e, _b=b: e.activation(
            dump[:], xc[_b][:], AF.Square, accum_out=ssq[:, 0:1]), waits=[t_xc])
        c_sc, c_b = k.cst(1.0 / 1024.0), k.cst(1e-5)
        t_std = k.step("scalar", lambda e, _s=c_sc, _b=c_b: e.activation(
            s1[:, 2:3], ssq[:, 0:1], AF.Sqrt, scale=_s, bias=_b),
            waits=[t_sq, k.ctok(1.0 / 1024.0), k.ctok(1e-5)])
        t_r = k.step("vector", lambda e: e.reciprocal(s1[:, 3:4], s1[:, 2:3]),
                     waits=[t_std])
        t_xn = k.step("vector", lambda e, _b=b: e.tensor_scalar_mul(
            xc[_b][:], xc[_b][:], s1[:, 3:4]), waits=[t_r])
        sb = i % 2
        cp_toks = []
        for j in range(8):
            pb = j % 2
            t_tr = k.step("tensor", lambda e, _b=b, _j=j, _pb=pb: e.transpose(
                ps[_pb][:, (_j // 2) * 128:(_j // 2) * 128 + 128],
                xc[_b][:, _j * 128:(_j + 1) * 128], idn[:]),
                waits=[t_xn, tok_idn, rel.get(("ps", pb))])
            t_cp = k.step("scalar", lambda e, _j=j, _pb=pb, _sb=sb: e.copy(
                stg[_sb][:, _j * 128:(_j + 1) * 128],
                ps[_pb][:, (_j // 2) * 128:(_j // 2) * 128 + 128]),
                waits=[t_tr, rel.get(("stg", sb))])
            rel[("ps", pb)] = t_cp
            cp_toks.append(t_cp)
        rel[("xc", b)] = cp_toks[-1]
        out_ap = bass.AP(xnT.tensor, i * 128,
                         [[T, 128], [128 * T, 8], [1, 128]])
        in_ap = bass.AP(stg[sb], 0, [[1024, 128], [128, 8], [1, 128]])
        t_st = k.dma("gpsimd" if i % 2 == 0 else "scalar", out_ap, in_ap,
                     waits=cp_toks)
        rel[("stg", sb)] = t_st




def build_p1(k, io, st):
    nc = k.nc
    xnT = io["xnT32"]
    g0T, maskT0, wexp_bf = io["g0T"], io["maskT0"], io["wexp0_bf"]

    weT = st.enter_context(nc.sbuf_tensor("p1_weT", [128, 8 * 8 * 1024], BF16))
    xnb = [st.enter_context(nc.sbuf_tensor(f"p1_xnb{i}", [128, 8 * 512], BF16)) for i in range(2)]
    xne = [st.enter_context(nc.sbuf_tensor(f"p1_xne{i}", [128, 8 * 512], BF16)) for i in range(2)]
    mrep = [st.enter_context(nc.sbuf_tensor(f"p1_mrep{i}", [128, 512], BF16)) for i in range(2)]
    bexp = st.enter_context(nc.sbuf_tensor("p1_bexp", [8, 1024], BF16))
    brow = st.enter_context(nc.sbuf_tensor("p1_brow", [8, 1], F32))
    rwn = st.enter_context(nc.sbuf_tensor("p1_rwn", [8, 1024], F32))
    rwT = st.enter_context(nc.sbuf_tensor("p1_rwT", [128, 64], F32))
    xn32 = [st.enter_context(nc.sbuf_tensor(f"p1_xn32{i}", [128, 512], F32)) for i in range(2)]
    lgT = st.enter_context(nc.sbuf_tensor("p1_lgT", [8, 512], F32))
    lgn = st.enter_context(nc.sbuf_tensor("p1_lgn", [128, 8], F32))
    m12 = st.enter_context(nc.sbuf_tensor("p1_m12", [128, 2], F32))
    msk1 = st.enter_context(nc.sbuf_tensor("p1_msk1", [128, 8], F32))
    masknat = st.enter_context(nc.sbuf_tensor("p1_masknat", [128, 8], BF16))
    maskT_sb = st.enter_context(nc.sbuf_tensor("p1_maskT", [8, T], BF16))
    g0st = [st.enter_context(nc.sbuf_tensor(f"p1_g0st{i}", [128, 512], BF16)) for i in range(2)]
    idn = st.enter_context(nc.sbuf_tensor("p1_idn", [128, 128], F32))
    idnb = st.enter_context(nc.sbuf_tensor("p1_idnb", [128, 128], BF16))

    wf = io["L0_ffn_w_experts"]
    cast_toks = [k.dma("gpsimd", wexp_bf[e], wf[e]) for e in range(8)]
    t_idn = k.dma("sync", idn[:], io["c_idn32"])
    t_idnb = k.dma("sync", idnb[:], io["c_idnb"])
    t_bexp = k.dma("gpsimd", bexp[:], io["L0_ffn_b_experts"])
    t_brow = k.dma("sync", brow[:], bass.AP(io["L0_ffn_b_router"].tensor, 0, [[1, 8], [1, 1]]))
    t_rwn = k.dma("sync", rwn[:], io["L0_ffn_w_router"])

    with ExitStack() as pst:
        psL = pst.enter_context(nc.psum_tensor("p1_psL", [8, 512], F32))
        psT = pst.enter_context(nc.psum_tensor("p1_psT", [128, 8], F32))
        psM = pst.enter_context(nc.psum_tensor("p1_psM", [8, 128], BF16))

        rwT_toks = []
        for dj in range(8):
            t_tr = k.step("tensor", lambda e, _dj=dj: e.transpose(
                psT[:], rwn[0:8, _dj * 128:(_dj + 1) * 128], idn[0:8, 0:8]),
                waits=[t_rwn, t_idn] + ([rwT_toks[-1]] if rwT_toks else []))
            t_cp = k.step("vector", lambda e, _dj=dj: e.tensor_copy(
                rwT[:, _dj * 8:(_dj + 1) * 8], psT[:]), waits=[t_tr])
            rwT_toks.append(t_cp)

        relx = {}
        t_lgT = None
        t_psL_free = None
        t_psT_free = rwT_toks[-1]
        t_psM_free = None
        mask_cp_toks = []
        for tc in range(8):
            mm_toks = []
            for dj in range(8):
                xb = dj % 2
                tl = k.dma("sync", xn32[xb][:],
                           xnT[dj * 128:(dj + 1) * 128, tc * 512:(tc + 1) * 512],
                           waits=[relx.get(("xn32", xb))])
                t_mm = k.step("tensor", lambda e, _dj=dj, _xb=xb: e.matmul(
                    psL[:], rwT[:, _dj * 8:(_dj + 1) * 8], xn32[_xb][:],
                    start=(_dj == 0), stop=(_dj == 7)),
                    waits=[tl, rwT_toks[-1], t_psL_free])
                relx[("xn32", xb)] = t_mm
                mm_toks.append(t_mm)
            t_lgT = k.step("vector", lambda e: e.tensor_scalar(
                lgT[:], psL[:], brow[:, 0:1], None, ALU.add),
                waits=[mm_toks[-1], t_brow])
            t_psL_free = t_lgT
            for q in range(4):
                tt = tc * 4 + q
                t_tq = k.step("tensor", lambda e, _q=q: e.transpose(
                    psT[:], lgT[0:8, _q * 128:(_q + 1) * 128], idn[0:8, 0:8]),
                    waits=[t_lgT, t_psT_free])
                t_cp = k.step("vector", lambda e: e.tensor_copy(lgn[:], psT[:]),
                              waits=[t_tq])
                t_psT_free = t_cp
                t_m1 = k.step("vector", lambda e: e.reduce_max(
                    m12[:, 0:1], lgn[:], axis=AX.X), waits=[t_cp])
                t_eq = k.step("vector", lambda e: e.tensor_scalar(
                    msk1[:], lgn[:], m12[:, 0:1], None, ALU.is_ge), waits=[t_m1])
                t_ms = k.step("vector", lambda e: e.scalar_tensor_tensor(
                    msk1[:], msk1[:], -1e30, lgn[:], ALU.mult, ALU.add), waits=[t_eq])
                t_m2 = k.step("vector", lambda e: e.reduce_max(
                    m12[:, 1:2], msk1[:], axis=AX.X), waits=[t_ms])
                t_mk = k.step("vector", lambda e: e.tensor_scalar(
                    masknat[:], lgn[:], m12[:, 1:2], None, ALU.is_ge), waits=[t_m2])
                t_tm = k.step("tensor", lambda e: e.transpose(
                    psM[:], masknat[:], idnb[:]), waits=[t_mk, t_idnb, t_psM_free])
                t_cm = k.step("vector", lambda e, _tt=tt: e.tensor_copy(
                    maskT_sb[:, _tt * 128:(_tt + 1) * 128], psM[:]), waits=[t_tm])
                t_psM_free = t_cm
                mask_cp_toks.append(t_cm)
        t_maskstore = k.dma("sync", maskT0, maskT_sb[:], waits=[mask_cp_toks[-1]])

    wet_toks = {}
    for e in range(8):
        for dj in range(8):
            wet_toks[(e, dj)] = k.dma(
                "scalar",
                weT[:, (e * 8 + dj) * 1024:(e * 8 + dj) * 1024 + 1024],
                wexp_bf[e][0:1024, dj * 128:(dj + 1) * 128],
                waits=[cast_toks[e]], transpose=True)
    wet_last = wet_toks[(7, 7)]

    with ExitStack() as pst:
        Y = pst.enter_context(nc.psum_tensor("p1_Y", [128, 4096], F32))
        rel = {}
        evac_toks = [None] * 8
        for tc in range(8):
            xb = tc % 2
            xn_toks = [k.dma("gpsimd", xnb[xb][:, dj * 512:(dj + 1) * 512],
                             xnT[dj * 128:(dj + 1) * 128, tc * 512:(tc + 1) * 512],
                             waits=[rel.get(("xnb", xb))])
                       for dj in range(8)]
            bias_toks = []
            for ok in range(8):
                t_b = k.step("tensor", lambda e, _ok=ok, _tc=tc: e.matmul(
                    Y[:, _ok * 512:(_ok + 1) * 512],
                    bexp[0:8, _ok * 128:(_ok + 1) * 128],
                    maskT_sb[0:8, _tc * 512:(_tc + 1) * 512],
                    start=True, stop=False),
                    waits=([t_bexp, mask_cp_toks[-1], evac_toks[ok]] if ok == 0 else
                           [evac_toks[ok]]),
                    inc=(ok == 7))
                bias_toks.append(t_b)
            last_mm = None
            for e in range(8):
                mb = e % 2
                t_mrep = k.dma("gpsimd", mrep[mb][:],
                               bass.AP(maskT0.tensor, e * T + tc * 512,
                                       [[0, 128], [1, 512]]),
                               waits=[t_maskstore, rel.get(("mrep", mb))])
                eb = e % 2
                tt_toks = []
                for dj in range(8):
                    t_tt = k.step("vector", lambda e2, _dj=dj, _mb=mb, _xb=xb, _eb=eb: e2.tensor_tensor(
                        xne[_eb][:, _dj * 512:(_dj + 1) * 512],
                        xnb[_xb][:, _dj * 512:(_dj + 1) * 512],
                        mrep[_mb][:], ALU.mult),
                        waits=[t_mrep, xn_toks[dj], rel.get(("xne", eb))],
                        inc=(dj == 7))
                    tt_toks.append(t_tt)
                t_xne = tt_toks[-1]
                rel[("mrep", mb)] = t_xne
                for ok in range(8):
                    for dj in range(8):
                        is_last = (e == 7 and dj == 7)
                        t_mm = k.step("tensor", lambda e2, _e=e, _ok=ok, _dj=dj, _lst=is_last, _eb=eb: e2.matmul(
                            Y[:, _ok * 512:(_ok + 1) * 512],
                            weT[:, (_e * 8 + _dj) * 1024 + _ok * 128:(_e * 8 + _dj) * 1024 + _ok * 128 + 128],
                            xne[_eb][:, _dj * 512:(_dj + 1) * 512],
                            start=False, stop=_lst),
                            waits=([t_xne, bias_toks[-1]] +
                                   [wet_toks[(e, d2)] for d2 in range(8)]
                                   if (ok == 0 and dj == 0) else []),
                            inc=(ok == 7 and dj == 7))
                        if t_mm is not None:
                            last_mm = t_mm
                rel[("xne", eb)] = last_mm
            rel[("xnb", xb)] = last_mm
            for ok in range(8):
                gb = ok % 2
                t_ev = k.step("scalar", lambda e, _ok=ok, _gb=gb: e.activation(
                    g0st[_gb][:], Y[:, _ok * 512:(_ok + 1) * 512], AF.Silu),
                    waits=[last_mm, rel.get(("g0st", gb))])
                evac_toks[ok] = t_ev
                t_st = k.dma("scalar", g0T[ok * 128:(ok + 1) * 128, tc * 512:(tc + 1) * 512],
                             g0st[gb][:], waits=[t_ev])
                rel[("g0st", gb)] = t_st




def build_p2(k, io, st):
    nc = k.nc
    xnT, g0T, x = io["xnT32"], io["g0T"], io["x"]
    out1, xn1T = io["out1"], io["xn1T32"]
    SEG = 1024
    NSEG = 4

    wbuf = st.enter_context(nc.sbuf_tensor("p2_wbuf", [128, 8 * 1024], BF16))
    wo = st.enter_context(nc.sbuf_tensor("p2_wo", [128, 8 * 1024], BF16))
    wgk2 = st.enter_context(nc.sbuf_tensor("p2_wgk2", [16, 512], BF16))
    bgk2pp = st.enter_context(nc.sbuf_tensor("p2_bgk2", [128, 4], F32))
    normw = st.enter_context(nc.sbuf_tensor("p2_normw", [128, 1], F32))
    xnb = [st.enter_context(nc.sbuf_tensor(f"p2_xnb{i}", [128, 8 * 512], BF16)) for i in range(2)]
    qT = st.enter_context(nc.sbuf_tensor("p2_qT", [128, 4 * 1024], BF16))
    kT = st.enter_context(nc.sbuf_tensor("p2_kT", [128, 4 * 1024], BF16))
    keT = st.enter_context(nc.sbuf_tensor("p2_keT", [128, 4 * 1024], BF16))
    vT = st.enter_context(nc.sbuf_tensor("p2_vT", [128, 8 * 1024], BF16))
    gsT = st.enter_context(nc.sbuf_tensor("p2_gsT", [128, 8 * 1024], BF16))
    t1sb = st.enter_context(nc.sbuf_tensor("p2_t1", [16, 1024], BF16))
    gkT = st.enter_context(nc.sbuf_tensor("p2_gkT", [128, 4 * 1024], F32))
    bT = st.enter_context(nc.sbuf_tensor("p2_bT", [128, 4 * 1024], F32))
    zeros64 = st.enter_context(nc.sbuf_tensor("p2_z64", [128, 64], F32))
    tmpa = st.enter_context(nc.sbuf_tensor("p2_tmpa", [128, 512], F32))
    tmpm = st.enter_context(nc.sbuf_tensor("p2_tmpm", [128, 512], F32))
    ebuf = st.enter_context(nc.sbuf_tensor("p2_ebuf", [128, 1024], BF16))
    dbuf = st.enter_context(nc.sbuf_tensor("p2_dbuf", [128, 1024], F32))
    blast = st.enter_context(nc.sbuf_tensor("p2_blast", [128, 64], F32))
    eblast = st.enter_context(nc.sbuf_tensor("p2_eblast", [128, 64], F32))
    Sf = st.enter_context(nc.sbuf_tensor("p2_Sf", [128, 4 * 128], F32))
    Sb = st.enter_context(nc.sbuf_tensor("p2_Sb", [128, 4 * 128], BF16))
    Asb = [st.enter_context(nc.sbuf_tensor(f"p2_A{i}", [64, 64], BF16)) for i in range(4)]
    vin = [st.enter_context(nc.sbuf_tensor(f"p2_vin{i}", [64, 128], BF16)) for i in range(4)]
    kdn = [st.enter_context(nc.sbuf_tensor(f"p2_kdn{i}", [64, 64], BF16)) for i in range(4)]
    osb = st.enter_context(nc.sbuf_tensor("p2_osb", [128, 512], BF16))
    osq = st.enter_context(nc.sbuf_tensor("p2_osq", [128, 512], BF16))
    r2 = st.enter_context(nc.sbuf_tensor("p2_r2", [1, 512], F32))
    t2 = st.enter_context(nc.sbuf_tensor("p2_t2", [128, 512], BF16))
    ofin = st.enter_context(nc.sbuf_tensor("p2_ofin", [128, 8 * 512], BF16))
    g0blk = st.enter_context(nc.sbuf_tensor("p2_g0blk", [128, 8 * 512], BF16))
    pT = st.enter_context(nc.sbuf_tensor("p2_pT", [128, 8 * 512], BF16))
    pnat = st.enter_context(nc.sbuf_tensor("p2_pnat", [128, 1024], F32))
    xt = st.enter_context(nc.sbuf_tensor("p2_xt", [128, 1024], F32))
    o1t = st.enter_context(nc.sbuf_tensor("p2_o1t", [128, 1024], F32))
    dump = st.enter_context(nc.sbuf_tensor("p2_dump", [128, 1024], F32))
    s1 = st.enter_context(nc.sbuf_tensor("p2_s1", [128, 4], F32))
    ssq = st.enter_context(nc.sbuf_tensor("p2_ssq", [128, 1], F32))
    stg = st.enter_context(nc.sbuf_tensor("p2_stg", [128, 1024], F32))
    idn = st.enter_context(nc.sbuf_tensor("p2_idn", [128, 128], F32))
    idnb = st.enter_context(nc.sbuf_tensor("p2_idnb", [128, 128], BF16))
    triu = st.enter_context(nc.sbuf_tensor("p2_triu", [64, 64], BF16))
    onesb = st.enter_context(nc.sbuf_tensor("p2_onesb", [128, 128], BF16))
    onesf = st.enter_context(nc.sbuf_tensor("p2_onesf", [1, 128], F32))

    psO = [st.enter_context(nc.psum_tensor(f"p2_psO{i}", [128, 512], F32)) for i in range(4)]
    psAB = st.enter_context(nc.psum_tensor("p2_psAB", [128, 512], F32))
    psS = st.enter_context(nc.psum_tensor("p2_psS", [128, 512], F32))
    psT = st.enter_context(nc.psum_tensor("p2_psT", [128, 256], BF16))
    psP = st.enter_context(nc.psum_tensor("p2_psP", [128, 512], F32))

    t_idn = k.dma("sync", idn[:], io["c_idn32"])
    t_idnb = k.dma("sync", idnb[:], io["c_idnb"])
    t_triu = k.dma("sync", triu[:], io["c_triu64"])
    t_onesb = k.dma("sync", onesb[:], io["c_ones128"])
    t_onesf = k.dma("sync", onesf[:], bass.AP(io["c_ones128f"].tensor, 0, [[128, 1], [1, 128]]))
    t_wgk2 = k.dma("gpsimd", wgk2[:], io["L0_attn_Wgk2"])
    t_bgk2 = k.dma("sync", bgk2pp[:], bass.AP(io["L0_attn_bgk2"].tensor, 0, [[1, 128], [128, 4]]),
                   allow_slow_non_contiguous=True)
    t_normw = k.dma("sync", normw[:], bass.AP(io["L0_attn_norm_w"].tensor, 0, [[1, 128], [1, 1]]))
    t_wo = [k.dma("gpsimd", wo[:, dv * 1024:(dv + 1) * 1024],
                  io["L0_attn_Wo"][dv * 128:(dv + 1) * 128, :]) for dv in range(8)]
    t_z = k.step("vector", lambda e: e.memset(zeros64[:], 0.0))
    t_S0 = k.step("vector", lambda e: e.memset(Sf[:], 0.0))
    t_Sb0 = k.step("vector", lambda e: e.memset(Sb[:], 0.0), waits=[t_S0])

    rel = {}
    t_Sbf = {h: t_Sb0 for h in range(8)}
    t_Supd = {h: t_S0 for h in range(8)}
    psO_rel = [None] * 4
    psP_rel = None
    psAB_rel = None
    psS_rel = None
    psT_rel = None
    t_wload = None

    def proj(name, wname, nfj, outbuf, outcols, silu, seg, xn_toks):
        nonlocal psP_rel, t_wload
        w_ap = io[wname]
        ncols = w_ap.shape[1]
        t_w = [k.dma("gpsimd", wbuf[:, dj * ncols:dj * ncols + ncols],
                     w_ap[dj * 128:(dj + 1) * 128, :], waits=[t_wload])
               for dj in range(8)]
        toks = []
        for tch in range(2):
            for fj in range(nfj):
                mms = []
                for dj in range(8):
                    t_mm = k.step("tensor", lambda e, _tch=tch, _fj=fj, _dj=dj, _nc=ncols: e.matmul(
                        psP[:], wbuf[:, _dj * _nc + _fj * 128:_dj * _nc + _fj * 128 + 128],
                        xnb[_tch][:, _dj * 512:(_dj + 1) * 512],
                        start=(_dj == 0), stop=(_dj == 7)),
                        waits=(t_w + xn_toks[tch] + [psP_rel] if dj == 0 else []),
                        inc=(dj == 7))
                    mms.append(t_mm)
                t_ev = k.step("scalar", lambda e, _tch=tch, _fj=fj: e.activation(
                    outbuf[:, _fj * outcols + _tch * 512:_fj * outcols + _tch * 512 + 512],
                    psP[:], AF.Silu if silu else AF.Copy),
                    waits=[mms[-1]])
                psP_rel = t_ev
                toks.append(t_ev)
        t_wload = toks[-1]
        return toks

    for seg in range(4):
        soff = seg * SEG
        xn_toks = []
        for tch in range(2):
            xb = tch
            xn_toks.append([k.dma("gpsimd", xnb[xb][:, dj * 512:(dj + 1) * 512],
                                  xnT[dj * 128:(dj + 1) * 128,
                                      soff + tch * 512:soff + tch * 512 + 512],
                                  waits=[rel.get(("xnb", xb))])
                           for dj in range(8)])

        v_toks = proj("v", "L0_attn_Wv", 8, vT, 1024, False, seg, xn_toks)
        g_toks = proj("g", "L0_attn_Wg", 8, gsT, 1024, True, seg, xn_toks)
        q_toks = proj("q", "L0_attn_Wq", 4, qT, 1024, False, seg, xn_toks)
        k_toks = proj("k", "L0_attn_Wk", 4, kT, 1024, False, seg, xn_toks)
        rel[("xnb", 0)] = k_toks[-1]
        rel[("xnb", 1)] = k_toks[-1]

        t_w1 = [k.dma("gpsimd", wbuf[:, dj * 16:dj * 16 + 16],
                      io["L0_attn_Wgk1"][dj * 128:(dj + 1) * 128, :],
                      waits=[t_wload]) for dj in range(8)]
        t1_toks = []
        for tch in range(2):
            mms = []
            for dj in range(8):
                t_mm = k.step("tensor", lambda e, _tch=tch, _dj=dj: e.matmul(
                    psP[0:16, :], wbuf[:, _dj * 16:_dj * 16 + 16],
                    xnb[_tch][:, _dj * 512:(_dj + 1) * 512],
                    start=(_dj == 0), stop=(_dj == 7)),
                    waits=(t_w1 + xn_toks[tch] + [psP_rel] if dj == 0 else []),
                    inc=(dj == 7))
                mms.append(t_mm)
            t_ev = k.step("scalar", lambda e, _tch=tch: e.activation(
                t1sb[:, _tch * 512:(_tch + 1) * 512], psP[0:16, :], AF.Copy),
                waits=[mms[-1]])
            psP_rel = t_ev
            t1_toks.append(t_ev)
        t_wload = t1_toks[-1]
        gk_toks = {}
        prev_ts = None
        for fj in range(4):
            for tch in range(2):
                mms = []
                for c2 in range(4):
                    pass
                t_mm = k.step("tensor", lambda e, _fj=fj, _tch=tch: e.matmul(
                    psP[:], wgk2[0:16, _fj * 128:(_fj + 1) * 128],
                    t1sb[0:16, _tch * 512:(_tch + 1) * 512],
                    start=True, stop=True),
                    waits=[t_wgk2, t1_toks[tch], psP_rel])
                t_min = k.step("vector", lambda e, _fj=fj: e.tensor_scalar(
                    tmpm[:], psP[:], bgk2pp[:, _fj:_fj + 1], 0.0, ALU.add, ALU.min),
                    waits=[t_mm, t_bgk2, gk_toks.get((fj, tch - 1)),
                           gk_toks.get((fj - 1, 1))])
                t_abs = k.step("scalar", lambda e, _fj=fj: e.activation(
                    tmpa[:], psP[:], AF.Abs, bias=bgk2pp[:, _fj:_fj + 1]),
                    waits=[t_mm, t_bgk2, prev_ts, t_min])
                cm1 = k.cst(-1.0)
                t_e = k.step("scalar", lambda e, _s=cm1: e.activation(
                    tmpa[:], tmpa[:], AF.Exp, scale=_s),
                    waits=[t_abs, k.ctok(-1.0)])
                c1 = k.cst(1.0)
                t_l = k.step("scalar", lambda e, _b=c1: e.activation(
                    tmpa[:], tmpa[:], AF.Ln, bias=_b), waits=[t_e, k.ctok(1.0)])
                psP_rel = [t_l, t_min]
                t_s = k.step("vector", lambda e: e.scalar_tensor_tensor(
                    tmpm[:], tmpa[:], -1.0, tmpm[:], ALU.mult, ALU.add),
                    waits=[t_l, t_min])
                t_gk = k.step("vector", lambda e, _fj=fj, _tch=tch: e.tensor_scalar_mul(
                    gkT[:, _fj * 1024 + _tch * 512:_fj * 1024 + _tch * 512 + 512],
                    tmpm[:], 1.0 / 16.0), waits=[t_s])
                gk_toks[(fj, tch)] = t_gk
                prev_ts = t_s

        b_toks = {}
        for fj in range(4):
            prev = None
            for c in range(16):
                t_b = k.step("vector", lambda e, _fj=fj, _c=c: e.tensor_tensor_scan(
                    bT[:, _fj * 1024 + _c * 64:_fj * 1024 + _c * 64 + 64],
                    zeros64[:],
                    gkT[:, _fj * 1024 + _c * 64:_fj * 1024 + _c * 64 + 64],
                    0.0, ALU.add, ALU.add),
                    waits=[t_z, gk_toks[(fj, c // 8)], prev])
                prev = t_b
            b_toks[fj] = prev
        bl_toks = {}
        for fj in range(4):
            t_bl = k.step("vector", lambda e, _fj=fj: e.tensor_copy(
                blast[:, _fj * 16:(_fj + 1) * 16],
                bass.AP(bT, _fj * 1024 + 63, [[4096, 128], [64, 16]])),
                waits=[b_toks[fj]])
            bl_toks[fj] = t_bl
        t_ebl = k.step("scalar", lambda e: e.activation(eblast[:], blast[:], AF.Exp),
                       waits=list(bl_toks.values()))
        qe_toks, ke_toks, kd_toks = {}, {}, {}
        prev_ted = None
        for fj in range(4):
            t_eb = k.step("scalar", lambda e, _fj=fj: e.activation(
                ebuf[:], bT[:, _fj * 1024:(_fj + 1) * 1024], AF.Exp),
                waits=[b_toks[fj], kd_toks.get(fj - 1)])
            t_qe = k.step("vector", lambda e, _fj=fj: e.scalar_tensor_tensor(
                qT[:, _fj * 1024:(_fj + 1) * 1024], ebuf[:], 0.125,
                qT[:, _fj * 1024:(_fj + 1) * 1024], ALU.mult, ALU.mult),
                waits=[t_eb, q_toks[-1]])
            qe_toks[fj] = t_qe
            cm1b = k.cst(-1.0)
            t_en = k.step("scalar", lambda e, _fj=fj, _s=cm1b: e.activation(
                ebuf[:], bT[:, _fj * 1024:(_fj + 1) * 1024], AF.Exp, scale=_s),
                waits=[t_qe, k.ctok(-1.0)])
            t_ke = k.step("vector", lambda e, _fj=fj: e.tensor_tensor(
                keT[:, _fj * 1024:(_fj + 1) * 1024], ebuf[:],
                kT[:, _fj * 1024:(_fj + 1) * 1024], ALU.mult),
                waits=[t_en, k_toks[-1]])
            ke_toks[fj] = t_ke
            d_toks = []
            for c in range(16):
                t_d = k.step("vector", lambda e, _fj=fj, _c=c: e.tensor_scalar(
                    dbuf[:, _c * 64:(_c + 1) * 64],
                    bT[:, _fj * 1024 + _c * 64:_fj * 1024 + _c * 64 + 64],
                    blast[:, _fj * 16 + _c:_fj * 16 + _c + 1], None, ALU.subtract),
                    waits=([bl_toks[fj], t_ke, prev_ted]
                           if c == 0 else [bl_toks[fj]]))
                d_toks.append(t_d)
            t_ed = k.step("scalar", lambda e, _s=cm1b: e.activation(
                ebuf[:], dbuf[:], AF.Exp, scale=_s),
                waits=d_toks + [k.ctok(-1.0)])
            t_kd = k.step("vector", lambda e, _fj=fj: e.tensor_tensor(
                kT[:, _fj * 1024:(_fj + 1) * 1024], ebuf[:],
                kT[:, _fj * 1024:(_fj + 1) * 1024], ALU.mult),
                waits=[t_ed])
            kd_toks[fj] = t_kd
            prev_ted = t_ed

        ho_last = {}
        for blk in range(2):
            boff = blk * 512
            for hg in range(2):
                heads = range(hg * 4, hg * 4 + 4)
                for c in range(8):
                    for h in heads:
                        fjq = h // 2
                        p0 = 64 * (h % 2)
                        ob = h % 4
                        bb = (c * 4 + h % 4) % 2
                        col = fjq * 1024 + boff + c * 64
                        vcol = h * 1024 + boff + c * 64
                        t_tv = k.step("tensor", lambda e, _vcol=vcol: e.transpose(
                            psT[0:64, 0:128], vT[:, _vcol:_vcol + 64], idnb[:]),
                            waits=[v_toks[-1], t_idnb, psT_rel])
                        t_tk = k.step("tensor", lambda e, _col=col, _p0=p0: e.transpose(
                            psT[0:64, 128:192],
                            kT[_p0:_p0 + 64, _col:_col + 64],
                            idnb[_p0:_p0 + 64, _p0:_p0 + 64]),
                            waits=[kd_toks[fjq]])
                        t_cv = k.step("scalar", lambda e, _bb=bb: e.activation(
                            vin[_bb][:], psT[0:64, 0:128], AF.Copy),
                            waits=[t_tv, t_tk, rel.get(("vin", bb))])
                        t_ck = k.step("scalar", lambda e, _bb=bb: e.activation(
                            kdn[_bb][:], psT[0:64, 128:192], AF.Copy),
                            waits=[t_tk, rel.get(("kdn", bb))])
                        psT_rel = t_ck
                        t_A = k.step("tensor", lambda e, _col=col, _p0=p0: e.matmul(
                            psAB[0:64, 0:64], keT[_p0:_p0 + 64, _col:_col + 64],
                            qT[_p0:_p0 + 64, _col:_col + 64], start=True, stop=True),
                            waits=[ke_toks[fjq], qe_toks[fjq], psAB_rel])
                        t_Am = k.step("vector", lambda e, _bb=bb: e.tensor_tensor(
                            Asb[_bb][:], psAB[0:64, 0:64], triu[:], ALU.mult),
                            waits=[t_A, t_triu, rel.get(("Asb", bb))])
                        psAB_rel = t_Am
                        t_o1 = k.step("tensor", lambda e, _bb=bb, _ob=ob, _c=c: e.matmul(
                            psO[_ob][:, _c * 64:(_c + 1) * 64], vin[_bb][:],
                            Asb[_bb][:], start=True, stop=False),
                            waits=[t_cv, t_Am, psO_rel[ob] if c == 0 else None])
                        t_o2 = k.step("tensor", lambda e, _bb=bb, _ob=ob, _c=c, _h=h, _col=col, _p0=p0: e.matmul(
                            psO[_ob][:, _c * 64:(_c + 1) * 64],
                            Sb[_p0:_p0 + 64, (_h // 2) * 128:(_h // 2 + 1) * 128],
                            qT[_p0:_p0 + 64, _col:_col + 64], start=False, stop=True),
                            waits=[t_Sbf[h]])
                        rel[("Asb", bb)] = t_o2
                        t_Sm = k.step("tensor", lambda e, _bb=bb, _p0=p0: e.matmul(
                            psS[_p0:_p0 + 64, 0:128], kdn[_bb][:], vin[_bb][:],
                            start=True, stop=True,
                            tile_position=(0, _p0)),
                            waits=[t_ck, t_cv, psS_rel, t_Supd[h]])
                        rel[("vin", bb)] = t_Sm
                        rel[("kdn", bb)] = t_Sm
                        gc = seg * 16 + blk * 8 + c
                        t_Su = k.step("vector", lambda e, _h=h, _fjq=fjq, _p0=p0, _lc=blk * 8 + c: e.scalar_tensor_tensor(
                            Sf[_p0:_p0 + 64, (_h // 2) * 128:(_h // 2 + 1) * 128],
                            Sf[_p0:_p0 + 64, (_h // 2) * 128:(_h // 2 + 1) * 128],
                            eblast[_p0:_p0 + 64, _fjq * 16 + _lc:_fjq * 16 + _lc + 1],
                            psS[_p0:_p0 + 64, 0:128], ALU.mult, ALU.add),
                            waits=[t_Sm, t_ebl, t_Supd[h]])
                        t_Supd[h] = t_Su
                        psS_rel = t_Su
                        t_Sb = k.step("vector", lambda e, _h=h, _p0=p0: e.tensor_copy(
                            Sb[_p0:_p0 + 64, (_h // 2) * 128:(_h // 2 + 1) * 128],
                            Sf[_p0:_p0 + 64, (_h // 2) * 128:(_h // 2 + 1) * 128]),
                            waits=[t_Su, t_o2])
                        t_Sbf[h] = t_Sb
                        ho_last[h] = t_o2
                for h in heads:
                    ob = h % 4
                    t_cpo = k.step("scalar", lambda e, _ob=ob: e.activation(
                        osb[:], psO[_ob][:], AF.Copy),
                        waits=[rel.get(("osb",)), ho_last[h]])
                    t_sq = k.step("scalar", lambda e, _ob=ob: e.activation(
                        osq[:], psO[_ob][:], AF.Square), waits=[rel.get(("osq",))])
                    t_ssq = k.step("tensor", lambda e: e.matmul(
                        psS[0:1, 0:512], onesb[:, 0:1], osq[:], start=True, stop=True),
                        waits=[t_sq, t_onesb, psS_rel])
                    c128 = k.cst(1.0 / 128.0)[0:1, 0:1]
                    ce = k.cst(1e-5)[0:1, 0:1]
                    cm5 = k.cst(-0.5)[0:1, 0:1]
                    t_ln = k.step("scalar", lambda e, _s=c128, _b=ce: e.activation(
                        r2[:], psS[0:1, 0:512], AF.Ln, scale=_s, bias=_b),
                        waits=[t_ssq, k.ctok(1.0 / 128.0), k.ctok(1e-5)])
                    t_ex = k.step("scalar", lambda e, _s=cm5: e.activation(
                        r2[:], r2[:], AF.Exp, scale=_s),
                        waits=[t_ln, k.ctok(-0.5)])
                    psS_rel = t_ex
                    t_bc = k.step("tensor", lambda e: e.matmul(
                        psAB[:, 0:512], onesf[0:1, :], r2[:], start=True, stop=True),
                        waits=[t_ex, t_onesf, psAB_rel])
                    t_n = k.step("vector", lambda e: e.scalar_tensor_tensor(
                        t2[:], osb[:], normw[:, 0:1], psAB[:, 0:512],
                        ALU.mult, ALU.mult),
                        waits=[t_bc, t_cpo, t_normw, rel.get(("t2",))])
                    psAB_rel = t_n
                    rel[("osb",)] = t_n
                    rel[("osq",)] = t_ssq
                    t_f = k.step("vector", lambda e, _h=h, _boff=boff: e.tensor_tensor(
                        ofin[:, _h * 512:(_h + 1) * 512], t2[:],
                        gsT[:, _h * 1024 + _boff:_h * 1024 + _boff + 512], ALU.mult),
                        waits=[t_n, g_toks[-1]])
                    rel[("t2",)] = t_f
                    rel[("ofin", h)] = [t_f, rel.get(("ofin_free",))]
                    psO_rel[ob] = t_sq
            g0_toks = [k.dma("sync", g0blk[:, oj * 512:(oj + 1) * 512],
                             g0T[oj * 128:(oj + 1) * 128, soff + boff:soff + boff + 512],
                             waits=[rel.get(("g0blk",))]) for oj in range(8)]
            pT_toks = []
            wo_mm_last = None
            for oj in range(8):
                mms = []
                for dvt in range(8):
                    t_mm = k.step("tensor", lambda e, _oj=oj, _dvt=dvt: e.matmul(
                        psP[:], wo[:, _dvt * 1024 + _oj * 128:_dvt * 1024 + _oj * 128 + 128],
                        ofin[:, _dvt * 512:(_dvt + 1) * 512],
                        start=(_dvt == 0), stop=(_dvt == 7)),
                        waits=(t_wo + [rel.get(("ofin", 7)), psP_rel] if dvt == 0 else []),
                        inc=(dvt == 7))
                    mms.append(t_mm)
                wo_mm_last = mms[-1]
                t_p = k.step("vector", lambda e, _oj=oj: e.tensor_tensor(
                    pT[:, _oj * 512:(_oj + 1) * 512], psP[:],
                    g0blk[:, _oj * 512:(_oj + 1) * 512], ALU.mult),
                    waits=[mms[-1], g0_toks[oj], rel.get(("pT",))])
                psP_rel = t_p
                pT_toks.append(t_p)
            rel[("g0blk",)] = pT_toks[-1]
            rel[("ofin_free",)] = wo_mm_last
            tr_last = None
            for tt in range(4):
                ti = (soff + boff) // 128 + tt
                t_x = k.dma("sync", xt[:], x[ti * 128:(ti + 1) * 128, :],
                            waits=[rel.get(("xt",))])
                pn_toks = []
                for oj in range(8):
                    pb = oj % 2
                    t_tr = k.step("tensor", lambda e, _oj=oj, _tt=tt, _pb=pb: e.transpose(
                        psT[:, _pb * 128:_pb * 128 + 128],
                        pT[:, _oj * 512 + _tt * 128:_oj * 512 + _tt * 128 + 128],
                        idnb[:]),
                        waits=[pT_toks[oj], t_idnb, psT_rel, rel.get(("psTp",))])
                    t_cp = k.step("vector", lambda e, _oj=oj, _pb=pb: e.tensor_copy(
                        pnat[:, _oj * 128:(_oj + 1) * 128],
                        psT[:, _pb * 128:_pb * 128 + 128]),
                        waits=[t_tr, rel.get(("pnat",))])
                    rel[("psTp",)] = t_cp
                    pn_toks.append(t_cp)
                    tr_last = t_tr
                psT_rel = pn_toks[-1]
                t_o1 = k.step("vector", lambda e: e.tensor_add(
                    o1t[:], xt[:], pnat[:]), waits=[t_x] + pn_toks + [rel.get(("o1t",))])
                rel[("pnat",)] = t_o1
                t_sto1 = k.dma("gpsimd", out1[ti * 128:(ti + 1) * 128, :], o1t[:],
                               waits=[t_o1])
                t_sum = k.step("vector", lambda e: e.reduce_sum(
                    s1[:, 0:1], o1t[:], axis=AX.X), waits=[t_o1])
                t_mean = k.step("vector", lambda e: e.tensor_scalar_mul(
                    s1[:, 1:2], s1[:, 0:1], 1.0 / 1024.0), waits=[t_sum])
                t_xc = k.step("vector", lambda e: e.tensor_scalar(
                    pnat[:], o1t[:], s1[:, 1:2], None, ALU.subtract),
                    waits=[t_mean] + pn_toks)
                t_sq2 = k.step("scalar", lambda e: e.activation(
                    dump[:], pnat[:], AF.Square, accum_out=ssq[:, 0:1]),
                    waits=[t_xc, rel.get(("dump",))])
                rel[("dump",)] = t_sq2
                c1024, ce5 = k.cst(1.0 / 1024.0), k.cst(1e-5)
                t_std = k.step("scalar", lambda e, _s=c1024, _b=ce5: e.activation(
                    s1[:, 2:3], ssq[:, 0:1], AF.Sqrt, scale=_s, bias=_b),
                    waits=[t_sq2, k.ctok(1.0 / 1024.0), k.ctok(1e-5)])
                t_r = k.step("vector", lambda e: e.reciprocal(s1[:, 3:4], s1[:, 2:3]),
                             waits=[t_std])
                t_xn1 = k.step("vector", lambda e: e.tensor_scalar_mul(
                    pnat[:], pnat[:], s1[:, 3:4]), waits=[t_r])
                rel[("o1t",)] = [t_xn1, t_sto1]
                cp2 = []
                for j in range(8):
                    sl = j % 4
                    t_tr2 = k.step("tensor", lambda e, _j=j, _sl=sl: e.transpose(
                        psAB[:, _sl * 128:_sl * 128 + 128],
                        pnat[:, _j * 128:(_j + 1) * 128], idn[:]),
                        waits=[t_xn1, t_idn, psAB_rel, rel.get(("psABt",))])
                    t_cp2 = k.step("scalar", lambda e, _j=j, _sl=sl: e.copy(
                        stg[:, _j * 128:(_j + 1) * 128],
                        psAB[:, _sl * 128:_sl * 128 + 128]),
                        waits=[t_tr2, rel.get(("stg",))])
                    rel[("psABt",)] = t_cp2
                    cp2.append(t_cp2)
                psAB_rel = cp2[-1]
                out_ap = bass.AP(xn1T.tensor, ti * 128,
                                 [[T, 128], [128 * T, 8], [1, 128]])
                in_ap = bass.AP(stg, 0, [[1024, 128], [128, 8], [1, 128]])
                t_st2 = k.dma("scalar", out_ap, in_ap, waits=cp2)
                rel[("stg",)] = t_st2
                rel[("xt",)] = t_o1
            rel[("pT",)] = tr_last




def build_p3(k, io, st):
    nc = k.nc
    xnT = io["xn1T32"]
    g1T, combT1 = io["g1T"], io["combT1"]
    w1_bf, w2_bf = io["w1_bf"], io["w2_bf"]

    w1eT = [st.enter_context(nc.sbuf_tensor(f"p3_w1eT{i}", [128, 8 * 2048], BF16)) for i in range(2)]
    w2eT = [st.enter_context(nc.sbuf_tensor(f"p3_w2eT{i}", [128, 16 * 1024], BF16)) for i in range(2)]
    xnb = [st.enter_context(nc.sbuf_tensor(f"p3_xnb{i}", [128, 8 * 512], BF16)) for i in range(2)]
    hmid = st.enter_context(nc.sbuf_tensor("p3_hmid", [128, 16 * 512], BF16))
    yacc = st.enter_context(nc.sbuf_tensor("p3_yacc", [128, 8 * 512], F32))
    crep = [st.enter_context(nc.sbuf_tensor(f"p3_crep{i}", [128, 512], BF16)) for i in range(2)]
    bexp2 = st.enter_context(nc.sbuf_tensor("p3_bexp2", [8, 1024], BF16))
    b1n = st.enter_context(nc.sbuf_tensor("p3_b1n", [16, 128], F32))
    b1pp = st.enter_context(nc.sbuf_tensor("p3_b1pp", [128, 8 * 16], F32))
    rwn = st.enter_context(nc.sbuf_tensor("p3_rwn", [8, 1024], F32))
    rwT = st.enter_context(nc.sbuf_tensor("p3_rwT", [128, 64], F32))
    xn32 = [st.enter_context(nc.sbuf_tensor(f"p3_xn32{i}", [128, 512], F32)) for i in range(2)]
    lgT = st.enter_context(nc.sbuf_tensor("p3_lgT", [8, 512], F32))
    lgn = st.enter_context(nc.sbuf_tensor("p3_lgn", [128, 8], F32))
    m12 = st.enter_context(nc.sbuf_tensor("p3_m12", [128, 8], F32))
    msk1 = st.enter_context(nc.sbuf_tensor("p3_msk1", [128, 8], F32))
    eq1 = st.enter_context(nc.sbuf_tensor("p3_eq1", [128, 8], F32))
    eq2 = st.enter_context(nc.sbuf_tensor("p3_eq2", [128, 8], F32))
    combb = st.enter_context(nc.sbuf_tensor("p3_combb", [128, 8], BF16))
    combT_sb = st.enter_context(nc.sbuf_tensor("p3_combT", [8, T], BF16))
    g1st = [st.enter_context(nc.sbuf_tensor(f"p3_g1st{i}", [128, 512], BF16)) for i in range(2)]
    tmp5 = st.enter_context(nc.sbuf_tensor("p3_tmp5", [128, 512], F32))
    idn = st.enter_context(nc.sbuf_tensor("p3_idn", [128, 128], F32))
    idnb = st.enter_context(nc.sbuf_tensor("p3_idnb", [128, 128], BF16))

    t_idn = k.dma("sync", idn[:], io["c_idn32"])
    t_idnb = k.dma("sync", idnb[:], io["c_idnb"])
    cast1, cast2 = {}, {}
    for e in range(8):
        cast1[e] = k.dma("gpsimd", w1_bf[e], io["L1_ffn_w1"][e])
        cast2[e] = k.dma("gpsimd", w2_bf[e], io["L1_ffn_w2"][e])
    t_bexp2 = k.dma("gpsimd", bexp2[:], io["L1_ffn_b2"])
    t_rwn = k.dma("sync", rwn[:], io["L1_ffn_router_w"])
    b1_toks = []
    with ExitStack() as pst:
        psB1 = pst.enter_context(nc.psum_tensor("p3_psB1", [128, 16], F32))
        psL = pst.enter_context(nc.psum_tensor("p3_psL", [8, 512], F32))
        psT = pst.enter_context(nc.psum_tensor("p3_psT", [128, 8], F32))
        psM = pst.enter_context(nc.psum_tensor("p3_psM", [8, 128], BF16))
        ps_rel = None
        for e in range(8):
            t_l = k.dma("sync", b1n[:], bass.AP(io["L1_ffn_b1"].tensor, e * 2048,
                                                [[128, 16], [1, 128]]),
                        waits=[b1_toks[-1] if b1_toks else None])
            t_tr = k.step("tensor", lambda e2, _e=e: e2.transpose(
                psB1[:], b1n[:], idn[0:16, 0:16]), waits=[t_l, t_idn, ps_rel])
            t_cp = k.step("vector", lambda e2, _e=e: e2.tensor_copy(
                b1pp[:, _e * 16:(_e + 1) * 16], psB1[:]), waits=[t_tr])
            ps_rel = t_cp
            b1_toks.append(t_cp)
        t_b1 = b1_toks[-1]

        rwT_toks = []
        t_psT_free = None
        for dj in range(8):
            t_tr = k.step("tensor", lambda e, _dj=dj: e.transpose(
                psT[:], rwn[0:8, _dj * 128:(_dj + 1) * 128], idn[0:8, 0:8]),
                waits=[t_rwn, t_idn, t_psT_free, ps_rel])
            t_cp = k.step("vector", lambda e, _dj=dj: e.tensor_copy(
                rwT[:, _dj * 8:(_dj + 1) * 8], psT[:]), waits=[t_tr])
            t_psT_free = t_cp
            rwT_toks.append(t_cp)
        relx = {}
        t_psL_free = None
        t_psM_free = None
        comb_cp_toks = []
        c_one = k.cst(1.0)
        for tc in range(8):
            mm_toks = []
            for dj in range(8):
                xb = dj % 2
                tl = k.dma("sync", xn32[xb][:],
                           xnT[dj * 128:(dj + 1) * 128, tc * 512:(tc + 1) * 512],
                           waits=[relx.get(("xn32", xb))])
                t_mm = k.step("tensor", lambda e, _dj=dj, _xb=xb: e.matmul(
                    psL[:], rwT[:, _dj * 8:(_dj + 1) * 8], xn32[_xb][:],
                    start=(_dj == 0), stop=(_dj == 7)),
                    waits=[tl, rwT_toks[-1], t_psL_free])
                relx[("xn32", xb)] = t_mm
                mm_toks.append(t_mm)
            t_lgT = k.step("vector", lambda e: e.tensor_copy(lgT[:], psL[:]),
                           waits=[mm_toks[-1]])
            t_psL_free = t_lgT
            for q in range(4):
                tt = tc * 4 + q
                t_tq = k.step("tensor", lambda e, _q=q: e.transpose(
                    psT[:], lgT[0:8, _q * 128:(_q + 1) * 128], idn[0:8, 0:8]),
                    waits=[t_lgT, t_psT_free])
                t_cp = k.step("vector", lambda e: e.tensor_copy(lgn[:], psT[:]),
                              waits=[t_tq])
                t_psT_free = t_cp
                t_m1 = k.step("vector", lambda e: e.reduce_max(
                    m12[:, 0:1], lgn[:], axis=AX.X), waits=[t_cp])
                t_eq = k.step("vector", lambda e: e.tensor_scalar(
                    eq1[:], lgn[:], m12[:, 0:1], None, ALU.is_ge), waits=[t_m1])
                t_ms = k.step("vector", lambda e: e.scalar_tensor_tensor(
                    msk1[:], eq1[:], -1e30, lgn[:], ALU.mult, ALU.add), waits=[t_eq])
                t_m2 = k.step("vector", lambda e: e.reduce_max(
                    m12[:, 1:2], msk1[:], axis=AX.X), waits=[t_ms])
                t_mk2 = k.step("vector", lambda e: e.tensor_scalar(
                    eq2[:], lgn[:], m12[:, 1:2], None, ALU.is_ge), waits=[t_m2])
                t_eq2 = k.step("vector", lambda e: e.tensor_sub(
                    eq2[:], eq2[:], eq1[:]), waits=[t_mk2])
                t_d = k.step("vector", lambda e: e.tensor_sub(
                    m12[:, 2:3], m12[:, 0:1], m12[:, 1:2]), waits=[t_m2])
                t_s12 = k.step("scalar", lambda e: e.activation(
                    m12[:, 3:4], m12[:, 2:3], AF.Sigmoid), waits=[t_d])
                t_s21 = k.step("vector", lambda e, _o=c_one: e.scalar_tensor_tensor(
                    m12[:, 4:5], m12[:, 3:4], -1.0, _o, ALU.mult, ALU.add),
                    waits=[t_s12, k.ctok(1.0)])
                t_c1 = k.step("vector", lambda e: e.tensor_scalar(
                    msk1[:], eq1[:], m12[:, 3:4], None, ALU.mult),
                    waits=[t_s12, t_eq2])
                t_cb = k.step("vector", lambda e: e.scalar_tensor_tensor(
                    combb[:], eq2[:], m12[:, 4:5], msk1[:], ALU.mult, ALU.add),
                    waits=[t_c1, t_s21])
                t_tm = k.step("tensor", lambda e: e.transpose(
                    psM[:], combb[:], idnb[:]), waits=[t_cb, t_idnb, t_psM_free])
                t_cm = k.step("vector", lambda e, _tt=tt: e.tensor_copy(
                    combT_sb[:, _tt * 128:(_tt + 1) * 128], psM[:]), waits=[t_tm])
                t_psM_free = t_cm
                comb_cp_toks.append(t_cm)
        t_combstore = k.dma("sync", combT1, combT_sb[:], waits=[comb_cp_toks[-1]])

    with ExitStack() as pst:
        psH = [pst.enter_context(nc.psum_tensor(f"p3_psH{i}", [128, 512], F32)) for i in range(2)]
        psW = [pst.enter_context(nc.psum_tensor(f"p3_psW{i}", [128, 512], F32)) for i in range(2)]
        psSd = pst.enter_context(nc.psum_tensor("p3_psSd", [128, 512], F32))
        rel = {}
        psH_rel = [comb_cp_toks[-1]] * 2
        psW_rel = [None] * 2
        psSd_rel = None
        wet_toks = {}

        def load_w(e, wb):
            toks = []
            for dj in range(8):
                toks.append(k.dma("sync",
                                  w1eT[wb][:, dj * 2048:(dj + 1) * 2048],
                                  w1_bf[e][0:2048, dj * 128:(dj + 1) * 128],
                                  waits=[cast1[e], rel.get(("w1eT", wb))],
                                  transpose=True))
            for fj in range(16):
                toks.append(k.dma("sync",
                                  w2eT[wb][:, fj * 1024:(fj + 1) * 1024],
                                  w2_bf[e][0:1024, fj * 128:(fj + 1) * 128],
                                  waits=[cast2[e], rel.get(("w2eT", wb))],
                                  transpose=True))
            return toks

        for tc in range(8):
            xb = tc % 2
            xn_toks = [k.dma("gpsimd", xnb[xb][:, dj * 512:(dj + 1) * 512],
                             xnT[dj * 128:(dj + 1) * 128, tc * 512:(tc + 1) * 512],
                             waits=[rel.get(("xnb", xb))])
                       for dj in range(8)]
            sd_toks = []
            for hk in range(8):
                t_sd = k.step("tensor", lambda e, _hk=hk, _tc=tc: e.matmul(
                    psSd[:], bexp2[0:8, _hk * 128:(_hk + 1) * 128],
                    combT_sb[0:8, _tc * 512:(_tc + 1) * 512],
                    start=True, stop=True),
                    waits=[t_bexp2, comb_cp_toks[-1], psSd_rel])
                t_cp = k.step("vector", lambda e, _hk=hk: e.tensor_copy(
                    yacc[:, _hk * 512:(_hk + 1) * 512], psSd[:]),
                    waits=[t_sd, rel.get(("yacc", hk))])
                psSd_rel = t_cp
                sd_toks.append(t_cp)
            for e in range(8):
                wb = e % 2
                wtoks = load_w(e, wb)
                cb = e % 2
                t_crep = k.dma("gpsimd", crep[cb][:],
                               bass.AP(combT1.tensor, e * T + tc * 512,
                                       [[0, 128], [1, 512]]),
                               waits=[t_combstore, rel.get(("crep", cb))])
                hm_toks = []
                for fj in range(16):
                    hb = fj % 2
                    mms = []
                    for dj in range(8):
                        t_mm = k.step("tensor", lambda e2, _fj=fj, _dj=dj, _wb=wb, _xb=xb, _hb=hb: e2.matmul(
                            psH[_hb][:],
                            w1eT[_wb][:, _dj * 2048 + _fj * 128:_dj * 2048 + _fj * 128 + 128],
                            xnb[_xb][:, _dj * 512:(_dj + 1) * 512],
                            start=(_dj == 0), stop=(_dj == 7)),
                            waits=(wtoks + xn_toks + [psH_rel[hb]] if dj == 0 else []),
                            inc=(dj == 7))
                        mms.append(t_mm)
                    t_g = k.step("scalar", lambda e2, _fj=fj, _e=e, _hb=hb: e2.activation(
                        hmid[:, _fj * 512:(_fj + 1) * 512], psH[_hb][:], AF.Gelu,
                        bias=b1pp[:, _e * 16 + _fj:_e * 16 + _fj + 1]),
                        waits=[mms[-1], t_b1, rel.get(("hmid", fj))])
                    psH_rel[hb] = t_g
                    t_sc = k.step("vector", lambda e2, _fj=fj, _cb=cb: e2.tensor_tensor(
                        hmid[:, _fj * 512:(_fj + 1) * 512],
                        hmid[:, _fj * 512:(_fj + 1) * 512],
                        crep[_cb][:], ALU.mult),
                        waits=[t_g, t_crep])
                    hm_toks.append(t_sc)
                rel[("crep", cb)] = hm_toks[-1]
                for hk in range(8):
                    wbk = hk % 2
                    mms = []
                    for fj in range(16):
                        w0 = [hm_toks[fj]] if hk == 0 else []
                        if fj == 0:
                            w0 = w0 + [psW_rel[wbk]]
                        t_mm = k.step("tensor", lambda e2, _hk=hk, _fj=fj, _wb=wb, _wbk=wbk: e2.matmul(
                            psW[_wbk][:],
                            w2eT[_wb][:, _fj * 1024 + _hk * 128:_fj * 1024 + _hk * 128 + 128],
                            hmid[:, _fj * 512:(_fj + 1) * 512],
                            start=(_fj == 0), stop=(_fj == 15)),
                            waits=w0,
                            inc=(fj == 15))
                        mms.append(t_mm)
                    t_add = k.step("vector", lambda e2, _hk=hk, _wbk=wbk: e2.tensor_tensor(
                        yacc[:, _hk * 512:(_hk + 1) * 512],
                        yacc[:, _hk * 512:(_hk + 1) * 512],
                        psW[_wbk][:], ALU.add),
                        waits=[mms[-1], sd_toks[hk], rel.get(("yaccw", hk))])
                    psW_rel[wbk] = t_add
                    rel[("yaccw", hk)] = t_add
                rel[("w1eT", wb)] = rel[("yaccw", 7)]
                rel[("w2eT", wb)] = rel[("yaccw", 7)]
                rel[("hmid_free", e)] = rel[("yaccw", 7)]
                for fj in range(16):
                    rel[("hmid", fj)] = rel[("yaccw", 7)]
            rel[("xnb", xb)] = rel[("yaccw", 7)]
            for hk in range(8):
                gb = hk % 2
                t_ev = k.step("scalar", lambda e, _hk=hk, _gb=gb: e.activation(
                    g1st[_gb][:], yacc[:, _hk * 512:(_hk + 1) * 512], AF.Silu),
                    waits=[rel.get(("yaccw", hk)), rel.get(("g1st", gb))])
                rel[("yacc", hk)] = t_ev
                t_st = k.dma("scalar", g1T[hk * 128:(hk + 1) * 128, tc * 512:(tc + 1) * 512],
                             g1st[gb][:], waits=[t_ev])
                rel[("g1st", gb)] = t_st






def build_p4(k, io, st):
    nc = k.nc
    xnT = io["xn1T32"]
    qhatT, khatT, Vn = io["qhatT"], io["khatT"], io["Vn"]
    xssmT, gateT = io["xssmT"], io["gateT"]
    wcq_bf, win_bf = io["wcq_bf"], io["win_bf"]

    wall = st.enter_context(nc.sbuf_tensor("p4_wall", [128, 8 * 5632], BF16))
    xnb = [st.enter_context(nc.sbuf_tensor(f"p4_xnb{i}", [128, 8 * 512], BF16)) for i in range(2)]
    qraw = st.enter_context(nc.sbuf_tensor("p4_qraw", [128, 8 * 512], BF16))
    kraw = st.enter_context(nc.sbuf_tensor("p4_kraw", [128, 2 * 512], BF16))
    vraw = st.enter_context(nc.sbuf_tensor("p4_vraw", [128, 2 * 512], BF16))
    xst = [st.enter_context(nc.sbuf_tensor(f"p4_xst{i}", [128, 512], BF16)) for i in range(2)]
    gst = [st.enter_context(nc.sbuf_tensor(f"p4_gst{i}", [128, 512], BF16)) for i in range(2)]
    qsq = st.enter_context(nc.sbuf_tensor("p4_qsq", [128, 512], BF16))
    r2 = st.enter_context(nc.sbuf_tensor("p4_r2", [1, 512], F32))
    qh = st.enter_context(nc.sbuf_tensor("p4_qh", [128, 512], BF16))
    rt1 = st.enter_context(nc.sbuf_tensor("p4_rt1", [128, 512], BF16))
    rt2 = st.enter_context(nc.sbuf_tensor("p4_rt2", [128, 512], BF16))
    vst = st.enter_context(nc.sbuf_tensor("p4_vst", [128, 4 * 64], BF16))
    gain_row = st.enter_context(nc.sbuf_tensor("p4_gain", [1, 32], F32))
    swap128 = st.enter_context(nc.sbuf_tensor("p4_swap", [128, 32], BF16))
    cs128 = st.enter_context(nc.sbuf_tensor("p4_cs", [128, T], F32))
    ss128 = st.enter_context(nc.sbuf_tensor("p4_ss", [128, T], F32))
    rep2 = st.enter_context(nc.sbuf_tensor("p4_rep2", [2, 128], F32))
    onesb = st.enter_context(nc.sbuf_tensor("p4_onesb", [128, 128], BF16))
    idnb = st.enter_context(nc.sbuf_tensor("p4_idnb", [128, 128], BF16))

    psP = [st.enter_context(nc.psum_tensor(f"p4_psP{i}", [128, 512], F32)) for i in range(2)]
    psR = st.enter_context(nc.psum_tensor("p4_psR", [1, 512], F32))
    psB = st.enter_context(nc.psum_tensor("p4_psB", [128, 512], F32))
    psSw = st.enter_context(nc.psum_tensor("p4_psSw", [128, 512], F32))
    psV = st.enter_context(nc.psum_tensor("p4_psV", [128, 64], BF16))

    t_cq = k.dma("gpsimd", wcq_bf, io["L1_attn_W_cq"])
    t_win = k.dma("gpsimd", win_bf, io["L1_attn_W_in"])
    t_cs = k.dma("sync", cs128[:], io["c_cs32"])
    t_ss = k.dma("sync", ss128[:], io["c_ss32"])
    t_rep2 = k.dma("sync", rep2[:], io["c_rep2"])
    t_ones = k.dma("sync", onesb[:], io["c_ones128"])
    t_idnb = k.dma("sync", idnb[:], io["c_idnb"])
    t_sw0 = k.dma("sync", swap128[0:32, :], io["c_swap32"])
    t_sw1 = k.dma("sync", swap128[64:96, :], io["c_swap32"])
    t_gain = k.dma("sync", gain_row[0:1, 0:16],
                   bass.AP(io["L1_attn_q_gain"].tensor, 0, [[16, 1], [1, 16]]))
    wd_toks = []
    for dj in range(8):
        wd_toks.append(k.dma("scalar", wall[:, dj * 5632:dj * 5632 + 4608],
                             win_bf[0:4608, dj * 128:(dj + 1) * 128],
                             waits=[t_win], transpose=True))
        wd_toks.append(k.dma("scalar", wall[:, dj * 5632 + 4608:dj * 5632 + 5632],
                             wcq_bf[0:1024, dj * 128:(dj + 1) * 128],
                             waits=[t_cq], transpose=True))
    t_g8 = k.step("vector", lambda e: e.tensor_scalar_mul(
        gain_row[0:1, 16:32], gain_row[0:1, 0:16], 0.125), waits=[t_gain])

    rel = {}
    eps32 = 1.1920929e-07
    for tc in range(8):
        xb = tc % 2
        xn_toks = [k.dma("gpsimd", xnb[xb][:, dj * 512:(dj + 1) * 512],
                         xnT[dj * 128:(dj + 1) * 128, tc * 512:(tc + 1) * 512],
                         waits=[rel.get(("xnb", xb))])
                   for dj in range(8)]
        proj_toks = {}
        for f in range(44):
            pb = f % 2
            mms = []
            for dj in range(8):
                t_mm = k.step("tensor", lambda e, _f=f, _dj=dj, _xb=xb, _pb=pb: e.matmul(
                    psP[_pb][:],
                    wall[:, _dj * 5632 + _f * 128:_dj * 5632 + _f * 128 + 128],
                    xnb[_xb][:, _dj * 512:(_dj + 1) * 512],
                    start=(_dj == 0), stop=(_dj == 7)),
                    waits=(wd_toks + xn_toks + [rel.get(("psP", pb))] if dj == 0 else []),
                    inc=(dj == 7))
                mms.append(t_mm)
            if f < 2:
                t_ev = k.step("scalar", lambda e, _f=f, _pb=pb: e.activation(
                    kraw[:, _f * 512:(_f + 1) * 512], psP[_pb][:], AF.Copy),
                    waits=[mms[-1], rel.get(("kraw",))])
            elif f < 4:
                t_ev = k.step("scalar", lambda e, _f=f, _pb=pb: e.activation(
                    vraw[:, (_f - 2) * 512:(_f - 1) * 512], psP[_pb][:], AF.Copy),
                    waits=[mms[-1], rel.get(("vraw",))])
            elif f < 20:
                sb = f % 2
                t_ev = k.step("scalar", lambda e, _pb=pb, _sb=sb: e.activation(
                    xst[_sb][:], psP[_pb][:], AF.Copy),
                    waits=[mms[-1], rel.get(("xst", sb))])
                t_st = k.dma("gpsimd", xssmT[(f - 4) * 128:(f - 3) * 128,
                                             tc * 512:(tc + 1) * 512],
                             xst[sb][:], waits=[t_ev])
                rel[("xst", sb)] = t_st
            elif f < 36:
                sb = f % 2
                t_ev = k.step("scalar", lambda e, _pb=pb, _sb=sb: e.activation(
                    gst[_sb][:], psP[_pb][:], AF.Silu),
                    waits=[mms[-1], rel.get(("gst", sb))])
                t_st = k.dma("gpsimd", gateT[(f - 20) * 128:(f - 19) * 128,
                                             tc * 512:(tc + 1) * 512],
                             gst[sb][:], waits=[t_ev])
                rel[("gst", sb)] = t_st
            else:
                t_ev = k.step("scalar", lambda e, _f=f, _pb=pb: e.activation(
                    qraw[:, (_f - 36) * 512:(_f - 35) * 512], psP[_pb][:], AF.Copy),
                    waits=[mms[-1], rel.get(("qraw",))])
            rel[("psP", pb)] = t_ev
            proj_toks[f] = t_ev

        def rms_rope(raw, nfj, out_hbm, gained, out_rel_key):
            for fj in range(nfj):
                t_sq = k.step("scalar", lambda e, _fj=fj, _r=raw: e.activation(
                    qsq[:], _r[:, _fj * 512:(_fj + 1) * 512], AF.Square),
                    waits=[proj_toks[43] if raw is qraw else proj_toks[1],
                           rel.get(("qsq",))])
                hb_toks = []
                for hh in range(2):
                    p0 = 64 * hh
                    h = fj * 2 + hh
                    t_mm = k.step("tensor", lambda e, _p0=p0: e.matmul(
                        psR[:], onesb[_p0:_p0 + 64, 0:1], qsq[_p0:_p0 + 64, :],
                        start=True, stop=True),
                        waits=[t_sq, t_ones, rel.get(("psR",))])
                    c64 = k.cst(1.0 / 64.0)[0:1, 0:1]
                    cep = k.cst(eps32)[0:1, 0:1]
                    cm5 = k.cst(-0.5)[0:1, 0:1]
                    t_ln = k.step("scalar", lambda e, _s=c64, _b=cep: e.activation(
                        r2[:], psR[:], AF.Ln, scale=_s, bias=_b),
                        waits=[t_mm, k.ctok(1.0 / 64.0), k.ctok(eps32),
                               rel.get(("r2",))])
                    t_ex = k.step("scalar", lambda e, _s=cm5: e.activation(
                        r2[:], r2[:], AF.Exp, scale=_s),
                        waits=[t_ln, k.ctok(-0.5)])
                    rel[("psR",)] = t_ex
                    if gained:
                        t_ex = k.step("vector", lambda e, _h=h: e.tensor_scalar(
                            r2[:], r2[:], gain_row[0:1, 16 + _h:16 + _h + 1],
                            None, ALU.mult), waits=[t_ex, t_g8])
                    t_bc = k.step("tensor", lambda e, _p0=p0: e.matmul(
                        psB[_p0:_p0 + 64, :], rep2[0:1, 0:64], r2[:],
                        start=True, stop=True, tile_position=(0, _p0)),
                        waits=[t_ex, t_rep2, rel.get(("psB", hh))])
                    rel[("r2",)] = t_bc
                    hb_toks.append(t_bc)
                t_qh = k.step("vector", lambda e, _fj=fj, _r=raw: e.tensor_tensor(
                    qh[:], _r[:, _fj * 512:(_fj + 1) * 512], psB[:], ALU.mult),
                    waits=hb_toks + [rel.get(("qh",))])
                rel[("psB", 0)] = t_qh
                rel[("psB", 1)] = t_qh
                rel[("qsq",)] = t_qh
                rope_toks = []
                for hh in range(2):
                    p0 = 64 * hh
                    t_sw = k.step("tensor", lambda e, _p0=p0: e.matmul(
                        psSw[_p0:_p0 + 32, :], swap128[_p0:_p0 + 32, :],
                        qh[_p0:_p0 + 32, :], start=True, stop=True,
                        tile_position=(_p0, _p0)),
                        waits=[t_qh, t_sw0, t_sw1, rel.get(("psSw", hh))])
                    t_r1 = k.step("vector", lambda e, _p0=p0, _tc=tc: e.tensor_tensor(
                        rt1[_p0:_p0 + 32, :], qh[_p0:_p0 + 32, :],
                        cs128[_p0:_p0 + 32, _tc * 512:(_tc + 1) * 512], ALU.mult),
                        waits=[t_qh, t_cs, rel.get(("rt1", hh))])
                    t_r2_ = k.step("vector", lambda e, _p0=p0, _tc=tc: e.tensor_tensor(
                        rt2[_p0:_p0 + 32, :], psSw[_p0:_p0 + 32, :],
                        ss128[_p0:_p0 + 32, _tc * 512:(_tc + 1) * 512], ALU.mult),
                        waits=[t_sw, t_ss, rel.get(("rt2", hh))])
                    rel[("psSw", hh)] = t_r2_
                    t_rsum = k.step("vector", lambda e, _p0=p0: e.tensor_add(
                        qh[_p0:_p0 + 32, :], rt1[_p0:_p0 + 32, :],
                        rt2[_p0:_p0 + 32, :]), waits=[t_r1, t_r2_])
                    rel[("rt1", hh)] = t_rsum
                    rel[("rt2", hh)] = t_rsum
                    rope_toks.append(t_rsum)
                t_out = k.dma("scalar", out_hbm[fj * 128:(fj + 1) * 128,
                                                tc * 512:(tc + 1) * 512],
                              qh[:], waits=rope_toks)
                rel[("qh",)] = t_out
        rms_rope(qraw, 8, qhatT, True, "q")
        rms_rope(kraw, 2, khatT, False, "k")
        rel[("qraw",)] = rel[("qh",)]
        rel[("kraw",)] = rel[("qh",)]
        for h in range(4):
            fjv = h // 2
            p0 = 64 * (h % 2)
            cps = []
            for blkt in range(4):
                t_tv = k.step("tensor", lambda e, _fjv=fjv, _p0=p0, _b=blkt: e.transpose(
                    psV[:, 0:64],
                    vraw[_p0:_p0 + 64, _fjv * 512 + _b * 128:_fjv * 512 + _b * 128 + 128],
                    idnb[_p0:_p0 + 64, _p0:_p0 + 64]),
                    waits=[proj_toks[3], t_idnb, rel.get(("psV",))])
                t_cp = k.step("vector", lambda e, _b=blkt: e.tensor_copy(
                    vst[:, _b * 64:(_b + 1) * 64], psV[:, 0:64]),
                    waits=[t_tv, rel.get(("vst",))])
                rel[("psV",)] = t_cp
                cps.append(t_cp)
            out_ap = bass.AP(Vn.tensor, h * T * 64 + tc * 512 * 64,
                             [[64, 128], [128 * 64, 4], [1, 64]])
            in_ap = bass.AP(vst, 0, [[256, 128], [64, 4], [1, 64]])
            t_stv = k.dma("scalar", out_ap, in_ap, waits=cps)
            rel[("vst",)] = t_stv
        rel[("vraw",)] = rel[("vst",)]
        rel[("xnb", xb)] = proj_toks[43]






def build_p5(k, io, st):
    nc = k.nc
    qhatT, khatT, Vn, attnT = io["qhatT"], io["khatT"], io["Vn"], io["attnT"]

    kk = st.enter_context(nc.sbuf_tensor("p5_kk", [65, T], BF16))
    Vp = st.enter_context(nc.sbuf_tensor("p5_Vp", [128, 32 * 65], BF16))
    qq = st.enter_context(nc.sbuf_tensor("p5_qq", [65, 4 * 512], BF16))
    Psb = [st.enter_context(nc.sbuf_tensor(f"p5_P{i}", [128, 1024], BF16)) for i in range(2)]
    stair = st.enter_context(nc.sbuf_tensor("p5_stair", [128, 4 * 512], BF16))
    negC = st.enter_context(nc.sbuf_tensor("p5_negC", [128, 16], F32))
    grow = st.enter_context(nc.sbuf_tensor("p5_grow", [1, 32], F32))
    onesf = st.enter_context(nc.sbuf_tensor("p5_onesf", [128, 128], F32))
    rr = st.enter_context(nc.sbuf_tensor("p5_rr", [128, 512], F32))
    ya = st.enter_context(nc.sbuf_tensor("p5_ya", [64, 512], BF16))
    oa = st.enter_context(nc.sbuf_tensor("p5_oa", [64, 512], BF16))

    t_st = k.dma("sync", stair[:],
                 bass.AP(io["c_stair"].tensor, 0, [[512, 128], [512 * 128, 4], [1, 512]]))
    t_onesf = k.dma("sync", onesf[:], io["c_ones128f"])
    t_gain = k.dma("sync", grow[0:1, 0:16],
                   bass.AP(io["L1_attn_q_gain"].tensor, 0, [[16, 1], [1, 16]]))
    c8 = k.cst(8.0)[0:1, 0:1]
    t_ab0 = k.step("scalar", lambda e, _s=c8: e.activation(
        grow[0:1, 16:32], grow[0:1, 0:16], AF.Abs, scale=_s),
        waits=[t_gain, k.ctok(8.0)])
    t_ab = k.step("vector", lambda e: e.tensor_scalar_mul(
        grow[0:1, 16:32], grow[0:1, 16:32], -1.0), waits=[t_ab0])
    with ExitStack() as pst:
        psN = pst.enter_context(nc.psum_tensor("p5_psN", [128, 16], F32))
        t_nm = k.step("tensor", lambda e: e.matmul(
            psN[:], onesf[0:1, :], grow[0:1, 16:32], start=True, stop=True),
            waits=[t_ab, t_onesf])
        t_nc = k.step("vector", lambda e: e.tensor_copy(negC[:], psN[:]), waits=[t_nm])
    t_kone = k.step("vector", lambda e: e.memset(kk[64:65, :], 1.0))

    psQK = [st.enter_context(nc.psum_tensor(f"p5_psQK{i}", [128, 1024], F32)) for i in range(2)]
    psYall = [st.enter_context(nc.psum_tensor(f"p5_psY{i}", [65, 512], F32)) for i in range(4)]

    rel = {}
    for kvh in range(4):
        t_kk = k.dma("sync", kk[0:64, :], khatT[kvh * 64:(kvh + 1) * 64, :],
                     waits=[rel.get(("kk",))])
        t_vset = k.step("gpsimd", lambda e: e.memset(Vp[:], 1.0),
                        waits=[rel.get(("Vp",))])
        in_ap = bass.AP(Vn.tensor, kvh * T * 64, [[64, 128], [128 * 64, 32], [1, 64]])
        out_ap = bass.AP(Vp, 0, [[2080, 128], [65, 32], [1, 64]])
        t_vp = k.dma("sync", out_ap, in_ap, waits=[t_vset])
        for hg in range(2):
            for sqg in range(8):
                par = sqg % 2
                psY = psYall[par * 2:par * 2 + 2]
                h0 = kvh * 4 + hg * 2
                q_toks = [k.dma("sync", qq[0:64, (par * 2 + i) * 512:(par * 2 + i + 1) * 512],
                                qhatT[(h0 + i) * 64:(h0 + i) * 64 + 64,
                                      sqg * 512:(sqg + 1) * 512],
                                waits=[rel.get(("qq", par))]) for i in range(2)]
                cr_toks = [k.step("vector", lambda e, _i=par * 2 + i, _hq=h0 + i: e.tensor_copy(
                    qq[64:65, _i * 512:(_i + 1) * 512],
                    bass.AP(negC, 64 * 16 + _hq, [[16, 1], [0, 512]])),
                    waits=[t_nc, rel.get(("qq64", par))]) for i in range(2)]
                nblk = 4 * sqg + 4

                def emit_qk(skb):
                    pb = skb % 2
                    t_qk = None
                    for i in range(2):
                        w = ([t_kk, t_kone] + q_toks + cr_toks +
                             [rel.get(("psQK", pb))]) if i == 0 else []
                        t_qk = k.step("tensor", lambda e, _skb=skb, _pb=pb, _i=i, _par=par: e.matmul(
                            psQK[_pb][:, _i * 512:(_i + 1) * 512],
                            kk[:, _skb * 128:(_skb + 1) * 128],
                            qq[:, (_par * 2 + _i) * 512:(_par * 2 + _i + 1) * 512],
                            start=True, stop=True),
                            waits=w)
                    return t_qk
                qk_tok = emit_qk(0)
                av_last = None
                for skb in range(nblk):
                    pb = skb % 2
                    t_e = k.step("scalar", lambda e, _pb=pb: e.activation(
                        Psb[_pb][:], psQK[_pb][:], AF.Exp),
                        waits=[qk_tok, rel.get(("Psb", pb))])
                    rel[("psQK", pb)] = t_e
                    if skb + 1 < nblk:
                        qk_tok = emit_qk(skb + 1)
                    m_toks = [t_e]
                    if skb >= 4 * sqg:
                        j = skb - 4 * sqg
                        m_toks = []
                        for i in range(2):
                            t_m = k.step("vector", lambda e, _i=i, _j=j, _pb=pb: e.tensor_tensor(
                                Psb[_pb][:, _i * 512:(_i + 1) * 512],
                                Psb[_pb][:, _i * 512:(_i + 1) * 512],
                                stair[:, _j * 512:(_j + 1) * 512], ALU.mult),
                                waits=[t_e, t_st])
                            m_toks.append(t_m)
                    av_toks = []
                    for i in range(2):
                        w = []
                        if i == 0:
                            w = [t_vp] + m_toks
                            if skb == 0:
                                w.append(rel.get(("psY", par)))
                        t_av = k.step("tensor", lambda e, _i=i, _skb=skb, _pb=pb, _first=(skb == 0), _last=(skb == nblk - 1): e.matmul(
                            psY[_i][:], Vp[:, _skb * 65:(_skb + 1) * 65],
                            Psb[_pb][:, _i * 512:(_i + 1) * 512],
                            start=_first, stop=_last),
                            waits=w)
                        av_toks.append(t_av)
                    av_last = av_toks[1]
                    rel[("Psb", pb)] = av_last
                ep_toks = []
                for i in range(2):
                    hq = h0 + i
                    t_cp = k.step("scalar", lambda e, _i=i: e.activation(
                        ya[:], psY[_i][0:64, :], AF.Copy),
                        waits=[av_last, rel.get(("ya",))])
                    t_rc = k.step("vector", lambda e, _i=i: e.reciprocal(
                        rr[64:65, :], psY[_i][64:65, :]),
                        waits=[av_last, rel.get(("rr",))])
                    t_bc = k.step("tensor", lambda e: e.matmul(
                        psQK[0][0:64, 0:512], onesf[64:65, 0:64], rr[64:65, :],
                        start=True, stop=True),
                        waits=[t_rc, t_onesf, rel.get(("psQK", 0)),
                               rel.get(("psB2",))])
                    rel[("rr",)] = t_bc
                    t_o = k.step("vector", lambda e: e.tensor_tensor(
                        oa[:], ya[:], psQK[0][0:64, 0:512], ALU.mult),
                        waits=[t_bc, t_cp, rel.get(("oa",))])
                    rel[("psB2",)] = t_o
                    rel[("ya",)] = t_o
                    t_so = k.dma("scalar", attnT[hq * 64:hq * 64 + 64,
                                                 sqg * 512:(sqg + 1) * 512],
                                 oa[:], waits=[t_o])
                    rel[("oa",)] = t_so
                    ep_toks.append(t_o)
                rel[("psY", par)] = ep_toks[-1]
                rel[("psQK", 0)] = ep_toks[-1]
                rel[("qq", par)] = av_last
                rel[("qq64", par)] = av_last
        rel[("kk",)] = av_last
        rel[("Vp",)] = av_last




def build_p6(k, io, st):
    nc = k.nc
    xssmT, gateT, attnT, g1T = io["xssmT"], io["gateT"], io["attnT"], io["g1T"]
    out1, out = io["out1"], io["out"]
    TG = 256
    NTG = T // TG

    wxT = st.enter_context(nc.sbuf_tensor("p6_wxT", [128, 16 * 96], BF16))
    wdtT = st.enter_context(nc.sbuf_tensor("p6_wdtT", [64, 16 * 128], BF16))
    wmT = st.enter_context(nc.sbuf_tensor("p6_wmT", [128, 16 * 1024], BF16))
    wpT = st.enter_context(nc.sbuf_tensor("p6_wpT", [128, 8 * 1024], BF16))
    A_sb = st.enter_context(nc.sbuf_tensor("p6_A", [128, 16 * 16], F32))
    dpp = st.enter_context(nc.sbuf_tensor("p6_dpp", [128, 16], F32))
    bdtpp = st.enter_context(nc.sbuf_tensor("p6_bdt", [128, 16], F32))
    cwpp = st.enter_context(nc.sbuf_tensor("p6_cw", [128, 16 * 4], F32))
    cbpp = st.enter_context(nc.sbuf_tensor("p6_cb", [128, 16], F32))
    wpp = st.enter_context(nc.sbuf_tensor("p6_wpp", [128, 2], F32))
    sel16 = st.enter_context(nc.sbuf_tensor("p6_sel", [16, 16 * 128], BF16))
    onesf = st.enter_context(nc.sbuf_tensor("p6_onesf", [128, 128], F32))
    idnb = st.enter_context(nc.sbuf_tensor("p6_idnb", [128, 128], BF16))
    idn32 = st.enter_context(nc.sbuf_tensor("p6_idn32", [128, 128], F32))
    state = st.enter_context(nc.sbuf_tensor("p6_state", [128, 256], F32))

    xin2 = [st.enter_context(nc.sbuf_tensor(f"p6_xin{i}", [128, TG + 3], BF16)) for i in range(2)]
    acc = [st.enter_context(nc.sbuf_tensor(f"p6_acc{i}", [128, TG], F32)) for i in range(2)]
    xs_bf = st.enter_context(nc.sbuf_tensor("p6_xs", [128, 16 * TG], BF16))
    dtlow = st.enter_context(nc.sbuf_tensor("p6_dtlow", [64, TG], BF16))
    Bsb = st.enter_context(nc.sbuf_tensor("p6_Bsb", [16, TG], BF16))
    Csb = st.enter_context(nc.sbuf_tensor("p6_Csb", [16, TG], BF16))
    delT = st.enter_context(nc.sbuf_tensor("p6_delT", [128, 16 * TG], F32))
    tmpm = st.enter_context(nc.sbuf_tensor("p6_tmpm", [128, TG], F32))
    tmpa = st.enter_context(nc.sbuf_tensor("p6_tmpa", [128, TG], F32))
    Brep = st.enter_context(nc.sbuf_tensor("p6_Brep", [128, 16 * TG], BF16))
    Crep = st.enter_context(nc.sbuf_tensor("p6_Crep", [128, 16 * TG], BF16))
    dA = st.enter_context(nc.sbuf_tensor("p6_dA", [128, 16 * TG], BF16))
    du = st.enter_context(nc.sbuf_tensor("p6_du", [128, TG], F32))
    d1 = st.enter_context(nc.sbuf_tensor("p6_d1", [128, 16 * TG], BF16))
    hs = st.enter_context(nc.sbuf_tensor("p6_hs", [128, 16 * TG], F32))
    zc = st.enter_context(nc.sbuf_tensor("p6_zc", [128, 16 * TG], BF16))
    zr1 = st.enter_context(nc.sbuf_tensor("p6_zr1", [128, 8 * TG], F32))
    ys = st.enter_context(nc.sbuf_tensor("p6_ys", [128, TG], F32))
    glb = st.enter_context(nc.sbuf_tensor("p6_glb", [128, 16 * TG], BF16))
    scano = st.enter_context(nc.sbuf_tensor("p6_scano", [128, 16 * TG], BF16))
    alb = st.enter_context(nc.sbuf_tensor("p6_alb", [128, 8 * TG], BF16))
    g1b = st.enter_context(nc.sbuf_tensor("p6_g1b", [128, 8 * TG], BF16))
    mer = st.enter_context(nc.sbuf_tensor("p6_mer", [128, 8 * TG], BF16))
    tmp2 = st.enter_context(nc.sbuf_tensor("p6_tmp2", [128, TG], F32))
    p1b = st.enter_context(nc.sbuf_tensor("p6_p1b", [128, 8 * TG], BF16))
    p1nat = st.enter_context(nc.sbuf_tensor("p6_p1nat", [128, 1024], F32))
    o1t = st.enter_context(nc.sbuf_tensor("p6_o1t", [128, 1024], F32))

    t_sel = k.dma("sync", sel16[:], io["c_sel16"])
    t_onesf = k.dma("sync", onesf[:], io["c_ones128f"])
    t_idnb = k.dma("sync", idnb[:], io["c_idnb"])
    t_idn32 = k.dma("sync", idn32[:], io["c_idn32"])
    t_S0 = k.step("vector", lambda e: e.memset(state[:], 0.0))

    wm_bf, wp_bf = io["wmout_bf"], io["wproj_bf"]
    t_cm = k.dma("gpsimd", wm_bf, io["L1_attn_W_mout"])
    t_cp_ = k.dma("gpsimd", wp_bf, io["L1_attn_W_proj"])
    wm_toks = [k.dma("scalar", wmT[:, i * 1024:(i + 1) * 1024],
                     wm_bf[0:1024, i * 128:(i + 1) * 128], waits=[t_cm],
                     transpose=True) for i in range(16)]
    wp_toks = [k.dma("scalar", wpT[:, i * 1024:(i + 1) * 1024],
                     wp_bf[0:1024, i * 128:(i + 1) * 128], waits=[t_cp_],
                     transpose=True) for i in range(8)]
    rel0 = {}
    with ExitStack() as pst:
        nat16 = pst.enter_context(nc.sbuf_tensor("p6_nat16", [16, 128], F32))
        natw = pst.enter_context(nc.sbuf_tensor("p6_natw", [96, 2048], BF16))
        natd = pst.enter_context(nc.sbuf_tensor("p6_natd", [128, 64], BF16))
        t_natw = k.dma("gpsimd", natw[:], io["L1_attn_W_xproj"])
        psW1 = pst.enter_context(nc.psum_tensor("p6_psW1", [128, 96], BF16))
        psW2 = pst.enter_context(nc.psum_tensor("p6_psW2", [64, 128], BF16))
        psT16 = pst.enter_context(nc.psum_tensor("p6_psT16", [128, 16], F32))
        for i in range(16):
            t_tr = k.step("tensor", lambda e, _i=i: e.transpose(
                psW1[:], natw[:, _i * 128:(_i + 1) * 128], idnb[0:96, 0:96]),
                waits=[t_natw, t_idnb, rel0.get("psW1")])
            t_c = k.step("vector", lambda e, _i=i: e.tensor_copy(
                wxT[:, _i * 96:(_i + 1) * 96], psW1[0:128, 0:96]), waits=[t_tr])
            rel0["psW1"] = t_c
        for i in range(16):
            t_l = k.dma("gpsimd", natd[:], io["L1_attn_W_dt"][i * 128:(i + 1) * 128, :],
                        waits=[rel0.get("natd")])
            t_tr = k.step("tensor", lambda e, _i=i: e.transpose(
                psW2[:], natd[:], idnb[:]), waits=[t_l, t_idnb, rel0.get("psW2")])
            t_c = k.step("vector", lambda e, _i=i: e.tensor_copy(
                wdtT[:, _i * 128:(_i + 1) * 128], psW2[:]), waits=[t_tr])
            rel0["psW2"] = t_c
            rel0["natd"] = t_c
        a_toks = [k.dma("sync", A_sb[:, i * 16:(i + 1) * 16],
                        io["L1_attn_A_log"][i * 128:(i + 1) * 128, :])
                  for i in range(16)]
        t_ae = k.step("scalar", lambda e: e.activation(
            A_sb[:], A_sb[:], AF.Exp), waits=a_toks)
        t_an = k.step("vector", lambda e: e.tensor_scalar_mul(
            A_sb[:], A_sb[:], -1.0), waits=[t_ae])

        def ppvec(src_ap, dst, col):
            t_l = k.dma("sync", nat16[:], src_ap, waits=[rel0.get("nat16")])
            t_tr = k.step("tensor", lambda e: e.transpose(
                psT16[:], nat16[:], idn32[0:16, 0:16]),
                waits=[t_l, t_idn32, rel0.get("psT16")])
            t_c = k.step("vector", lambda e, _c=col, _d=dst: e.tensor_copy(
                _d[:, _c:_c + 16], psT16[:]), waits=[t_tr])
            rel0["nat16"] = t_c
            rel0["psT16"] = t_c
            return t_c
        t_dpp = ppvec(bass.AP(io["L1_attn_D_param"].tensor, 0, [[128, 16], [1, 128]]), dpp, 0)
        t_bdt = ppvec(bass.AP(io["L1_attn_b_dt"].tensor, 0, [[128, 16], [1, 128]]), bdtpp, 0)
        t_cb = ppvec(bass.AP(io["L1_attn_conv_b"].tensor, 0, [[128, 16], [1, 128]]), cbpp, 0)
        cw_toks = [k.dma("sync", cwpp[:, i * 4:(i + 1) * 4],
                         bass.AP(io["L1_attn_conv_w"].tensor, i * 512, [[4, 128], [1, 4]]))
                   for i in range(16)]
        t_al = k.dma("sync", wpp[0:1, 0:1],
                     bass.AP(io["L1_attn_merge_alpha"].tensor, 0, [[1, 1], [1, 1]]))
        t_sg = k.step("scalar", lambda e: e.activation(
            wpp[0:1, 0:1], wpp[0:1, 0:1], AF.Sigmoid), waits=[t_al])
        t_wb = k.step("tensor", lambda e: e.matmul(
            psT16[:, 0:1], onesf[0:1, :], wpp[0:1, 0:1],
            start=True, stop=True), waits=[t_sg, t_onesf, rel0.get("psT16")])
        t_wc = k.step("vector", lambda e: e.tensor_copy(wpp[:, 0:1], psT16[:, 0:1]),
                      waits=[t_wb])
        c_one6 = k.cst(1.0)
        t_w1m = k.step("vector", lambda e, _o=c_one6: e.scalar_tensor_tensor(
            wpp[:, 1:2], wpp[:, 0:1], -1.0, _o, ALU.mult, ALU.add),
            waits=[t_wc, k.ctok(1.0)])

    psDL = st.enter_context(nc.psum_tensor("p6_psDL", [64, TG], F32))
    psBC = st.enter_context(nc.psum_tensor("p6_psBC", [16, 2 * TG], F32))
    psD2 = st.enter_context(nc.psum_tensor("p6_psD2", [128, TG], F32))
    psRep2 = [st.enter_context(nc.psum_tensor(f"p6_psRep{i}", [128, TG], F32)) for i in range(2)]
    psM2 = [st.enter_context(nc.psum_tensor(f"p6_psM2_{i}", [128, TG], F32)) for i in range(2)]
    psTr = st.enter_context(nc.psum_tensor("p6_psTr", [128, 128], BF16))

    rel = {("psDL",): t_w1m, ("psB_",): t_w1m, ("psC_",): t_w1m,
           ("psD2",): t_w1m, ("psRep", 0): t_w1m, ("psRep", 1): t_w1m}
    for tg in range(NTG):
        gl_toks = [k.dma("gpsimd", glb[:, dt * TG:(dt + 1) * TG],
                         gateT[dt * 128:(dt + 1) * 128, tg * TG:(tg + 1) * TG],
                         waits=[rel.get(("glb",))]) for dt in range(16)]
        al_toks = [k.dma("gpsimd", alb[:, oj * TG:(oj + 1) * TG],
                         attnT[oj * 128:(oj + 1) * 128, tg * TG:(tg + 1) * TG],
                         waits=[rel.get(("alb",))]) for oj in range(8)]
        g1_toks = [k.dma("gpsimd", g1b[:, pj * TG:(pj + 1) * TG],
                         g1T[pj * 128:(pj + 1) * 128, tg * TG:(tg + 1) * TG],
                         waits=[rel.get(("g1b",))]) for pj in range(8)]
        xs_toks = []
        for dt in range(16):
            xp = dt % 2
            xin = xin2[xp]
            if tg == 0:
                t_z = k.step("vector", lambda e, _xp=xp: e.memset(xin2[_xp][:, 0:3], 0.0),
                             waits=[rel.get(("xin", xp))])
                t_x = k.dma("sync", xin[:, 3:TG + 3],
                            xssmT[dt * 128:(dt + 1) * 128, 0:TG], waits=[t_z])
            else:
                t_x = k.dma("sync", xin[:, 0:TG + 3],
                            xssmT[dt * 128:(dt + 1) * 128, tg * TG - 3:tg * TG + TG],
                            waits=[rel.get(("xin", xp))])
            t_a0 = k.step("vector", lambda e, _dt=dt, _xp=xp: e.tensor_scalar(
                acc[0][:], xin2[_xp][:, 0:TG], cwpp[:, _dt * 4:_dt * 4 + 1],
                None, ALU.mult),
                waits=[t_x, cw_toks[dt], rel.get(("acc0",))])
            t_a1 = k.step("vector", lambda e, _dt=dt, _xp=xp: e.scalar_tensor_tensor(
                acc[1][:], xin2[_xp][:, 1:TG + 1], cwpp[:, _dt * 4 + 1:_dt * 4 + 2],
                acc[0][:], ALU.mult, ALU.add), waits=[t_a0, rel.get(("acc1",))])
            t_a2 = k.step("vector", lambda e, _dt=dt, _xp=xp: e.scalar_tensor_tensor(
                acc[0][:], xin2[_xp][:, 2:TG + 2], cwpp[:, _dt * 4 + 2:_dt * 4 + 3],
                acc[1][:], ALU.mult, ALU.add), waits=[t_a1])
            t_a3 = k.step("vector", lambda e, _dt=dt, _xp=xp: e.scalar_tensor_tensor(
                acc[1][:], xin2[_xp][:, 3:TG + 3], cwpp[:, _dt * 4 + 3:_dt * 4 + 4],
                acc[0][:], ALU.mult, ALU.add), waits=[t_a2])
            rel[("acc0",)] = t_a3
            t_xs = k.step("scalar", lambda e, _dt=dt: e.activation(
                xs_bf[:, _dt * TG:(_dt + 1) * TG], acc[1][:], AF.Silu,
                bias=cbpp[:, _dt:_dt + 1]),
                waits=[t_a3, t_cb, rel.get(("xs", dt))])
            rel[("acc1",)] = t_xs
            rel[("xin", xp)] = t_a3
            xs_toks.append(t_xs)
        mmdl, mmb, mmc = [], [], []
        for dt in range(16):
            w0 = [t_an, xs_toks[15], rel.get(("psDL",))] if dt == 0 else []
            mmdl.append(k.step("tensor", lambda e, _dt=dt: e.matmul(
                psDL[:], wxT[:, _dt * 96:_dt * 96 + 64],
                xs_bf[:, _dt * TG:(_dt + 1) * TG],
                start=(_dt == 0), stop=(_dt == 15)),
                waits=w0 + ([rel0["psW1"]] if dt == 0 else []), inc=(dt == 15)))
        for dt in range(16):
            w0 = [rel.get(("psB_",))] if dt == 0 else []
            mmb.append(k.step("tensor", lambda e, _dt=dt: e.matmul(
                psBC[:, 0:TG], wxT[:, _dt * 96 + 64:_dt * 96 + 80],
                xs_bf[:, _dt * TG:(_dt + 1) * TG],
                start=(_dt == 0), stop=(_dt == 15)), waits=w0, inc=(dt == 15)))
        for dt in range(16):
            w0 = [rel.get(("psC_",))] if dt == 0 else []
            mmc.append(k.step("tensor", lambda e, _dt=dt: e.matmul(
                psBC[:, TG:2 * TG], wxT[:, _dt * 96 + 80:_dt * 96 + 96],
                xs_bf[:, _dt * TG:(_dt + 1) * TG],
                start=(_dt == 0), stop=(_dt == 15)), waits=w0, inc=(dt == 15)))
        t_edl = k.step("scalar", lambda e: e.activation(
            dtlow[:], psDL[:], AF.Copy), waits=[mmdl[-1], rel.get(("dtlow",))])
        t_eb = k.step("scalar", lambda e: e.activation(
            Bsb[:], psBC[:, 0:TG], AF.Copy),
            waits=[mmb[-1], mmc[-1], rel.get(("Bsb",))])
        t_ec = k.step("scalar", lambda e: e.activation(
            Csb[:], psBC[:, TG:2 * TG], AF.Copy), waits=[t_eb, rel.get(("Csb",))])
        rel[("psDL",)] = t_edl
        rel[("psB_",)] = [t_eb, t_ec]
        rel[("psC_",)] = [t_eb, t_ec]
        rep_toks = []
        prev_rep = [rel.get(("psRep", 0)), rel.get(("psRep", 1))]
        for n in range(16):
            rp = n % 2
            t_m = k.step("tensor", lambda e, _n=n, _rp=rp: e.matmul(
                psRep2[_rp][:], sel16[:, _n * 128:(_n + 1) * 128], Bsb[:],
                start=True, stop=True), waits=[t_eb, t_sel, prev_rep[rp]])
            t_c = k.step("scalar", lambda e, _n=n, _rp=rp: e.activation(
                Brep[:, _n * TG:(_n + 1) * TG], psRep2[_rp][:], AF.Copy),
                waits=[t_m, rel.get(("Brep",))])
            prev_rep[rp] = t_c
            rep_toks.append(t_c)
        for n in range(16):
            rp = n % 2
            t_m = k.step("tensor", lambda e, _n=n, _rp=rp: e.matmul(
                psRep2[_rp][:], sel16[:, _n * 128:(_n + 1) * 128], Csb[:],
                start=True, stop=True), waits=[t_ec, prev_rep[rp]])
            t_c = k.step("scalar", lambda e, _n=n, _rp=rp: e.activation(
                Crep[:, _n * TG:(_n + 1) * TG], psRep2[_rp][:], AF.Copy),
                waits=[t_m, rel.get(("Crep",))])
            prev_rep[rp] = t_c
            rep_toks.append(t_c)
        rel[("psRep", 0)] = prev_rep[0]
        rel[("psRep", 1)] = prev_rep[1]
        t_brep, t_crep = rep_toks[15], rep_toks[31]
        sc_toks = []
        for dt in range(16):
            t_mm = k.step("tensor", lambda e, _dt=dt: e.matmul(
                psD2[:], wdtT[0:64, _dt * 128:(_dt + 1) * 128], dtlow[:],
                start=True, stop=True),
                waits=[t_edl, rel0["psW2"], rel.get(("psD2",))])
            t_min = k.step("vector", lambda e, _dt=dt: e.tensor_scalar(
                tmpm[:], psD2[:], bdtpp[:, _dt:_dt + 1], 0.0, ALU.add, ALU.max),
                waits=[t_mm, t_bdt, rel.get(("tmpm",))])
            t_abs = k.step("scalar", lambda e, _dt=dt: e.activation(
                tmpa[:], psD2[:], AF.Abs, bias=bdtpp[:, _dt:_dt + 1]),
                waits=[t_mm, t_min, rel.get(("tmpa",))])
            cm16 = k.cst(-1.0)
            c16_ = k.cst(1.0)
            t_e = k.step("scalar", lambda e, _s=cm16: e.activation(
                tmpa[:], tmpa[:], AF.Exp, scale=_s), waits=[t_abs, k.ctok(-1.0)])
            t_l = k.step("scalar", lambda e, _b=c16_: e.activation(
                tmpa[:], tmpa[:], AF.Ln, bias=_b), waits=[t_e, k.ctok(1.0)])
            rel[("psD2",)] = [t_l, t_min]
            t_del = k.step("vector", lambda e, _dt=dt: e.tensor_add(
                delT[:, _dt * TG:(_dt + 1) * TG], tmpm[:], tmpa[:]),
                waits=[t_l, t_min, rel.get(("delT", dt))])
            rel[("tmpm",)] = t_del
            rel[("tmpa",)] = t_del
            da_toks = []
            for n in range(16):
                t_da = k.step("scalar", lambda e, _dt=dt, _n=n: e.activation(
                    dA[:, _n * TG:(_n + 1) * TG],
                    delT[:, _dt * TG:(_dt + 1) * TG], AF.Exp,
                    scale=A_sb[:, _dt * 16 + _n:_dt * 16 + _n + 1]),
                    waits=[t_del, t_an, rel.get(("dA", n))])
                da_toks.append(t_da)
            t_du = k.step("vector", lambda e, _dt=dt: e.tensor_tensor(
                du[:], delT[:, _dt * TG:(_dt + 1) * TG],
                xs_bf[:, _dt * TG:(_dt + 1) * TG], ALU.mult),
                waits=[t_del, xs_toks[dt], rel.get(("du",))])
            t_d1 = k.step("vector", lambda e: e.tensor_tensor(
                bass.AP(d1, 0, [[16 * TG, 128], [TG, 16], [1, TG]]),
                bass.AP(du, 0, [[TG, 128], [0, 16], [1, TG]]),
                bass.AP(Brep, 0, [[16 * TG, 128], [TG, 16], [1, TG]]), ALU.mult),
                waits=[t_du, t_brep, rel.get(("d1",))])
            rel[("du",)] = t_d1
            scan_toks = []
            for n in range(16):
                t_s = k.step("vector", lambda e, _dt=dt, _n=n: e.tensor_tensor_scan(
                    hs[:, _n * TG:(_n + 1) * TG], dA[:, _n * TG:(_n + 1) * TG],
                    d1[:, _n * TG:(_n + 1) * TG],
                    state[:, _dt * 16 + _n:_dt * 16 + _n + 1],
                    ALU.mult, ALU.add),
                    waits=[da_toks[n], t_d1, t_S0, rel.get(("hs", n)),
                           rel.get(("state", dt))])
                scan_toks.append(t_s)
            for n in range(16):
                rel[("dA", n)] = scan_toks[n]
            rel[("d1",)] = scan_toks[-1]
            t_su = k.step("vector", lambda e, _dt=dt: e.tensor_copy(
                state[:, _dt * 16:(_dt + 1) * 16],
                bass.AP(hs, TG - 1, [[16 * TG, 128], [TG, 16]])),
                waits=scan_toks)
            rel[("state", dt)] = t_su
            t_z = k.step("vector", lambda e: e.tensor_tensor(
                zc[:], hs[:], Crep[:], ALU.mult),
                waits=scan_toks + [t_crep, rel.get(("zc",))])
            for n in range(16):
                rel[("hs", n)] = t_z
            t_r1 = k.step("vector", lambda e: e.tensor_add(
                zr1[:], zc[:, 0:8 * TG], zc[:, 8 * TG:16 * TG]),
                waits=[t_z, rel.get(("zr1",))])
            rel[("zc",)] = t_r1
            t_r2 = k.step("vector", lambda e: e.tensor_add(
                zr1[:, 0:4 * TG], zr1[:, 0:4 * TG], zr1[:, 4 * TG:8 * TG]),
                waits=[t_r1])
            t_r3 = k.step("vector", lambda e: e.tensor_add(
                zr1[:, 0:2 * TG], zr1[:, 0:2 * TG], zr1[:, 2 * TG:4 * TG]),
                waits=[t_r2])
            t_r4 = k.step("vector", lambda e: e.tensor_add(
                ys[:], zr1[:, 0:TG], zr1[:, TG:2 * TG]),
                waits=[t_r3, rel.get(("ys",))])
            rel[("zr1",)] = t_r4
            t_y2 = k.step("vector", lambda e, _dt=dt: e.scalar_tensor_tensor(
                tmp2[:], xs_bf[:, _dt * TG:(_dt + 1) * TG], dpp[:, _dt:_dt + 1],
                ys[:], ALU.mult, ALU.add),
                waits=[t_r4, t_dpp, rel.get(("tmp2",))])
            rel[("ys",)] = t_y2
            t_sc = k.step("vector", lambda e, _dt=dt: e.tensor_tensor(
                scano[:, _dt * TG:(_dt + 1) * TG], tmp2[:],
                glb[:, _dt * TG:(_dt + 1) * TG], ALU.mult),
                waits=[t_y2, gl_toks[dt], rel.get(("scano", dt))])
            rel[("tmp2",)] = t_sc
            sc_toks.append(t_sc)
        for dt in range(16):
            rel[("delT", dt)] = sc_toks[-1]
        mer_toks = []
        for oj in range(8):
            mb = oj % 2
            mms = []
            for dt in range(16):
                w0 = []
                if dt == 0:
                    w0 = sc_toks + wm_toks + [rel.get(("psM2", mb))]
                mms.append(k.step("tensor", lambda e, _oj=oj, _dt=dt, _mb=mb: e.matmul(
                    psM2[_mb][:], wmT[:, _dt * 1024 + _oj * 128:_dt * 1024 + _oj * 128 + 128],
                    scano[:, _dt * TG:(_dt + 1) * TG],
                    start=(_dt == 0), stop=(_dt == 15)), waits=w0, inc=(dt == 15)))
            t_t1 = k.step("vector", lambda e, _mb=mb: e.tensor_scalar(
                tmp2[:], psM2[_mb][:], wpp[:, 1:2], None, ALU.mult),
                waits=[mms[-1], t_w1m, rel.get(("tmp2",))])
            t_mer = k.step("vector", lambda e, _oj=oj: e.scalar_tensor_tensor(
                mer[:, _oj * TG:(_oj + 1) * TG], alb[:, _oj * TG:(_oj + 1) * TG],
                wpp[:, 0:1], tmp2[:], ALU.mult, ALU.add),
                waits=[t_t1, al_toks[oj], rel.get(("mer", oj))])
            rel[("psM2", mb)] = t_mer
            rel[("tmp2",)] = t_mer
            mer_toks.append(t_mer)
        rel[("alb",)] = mer_toks[-1]
        rel[("glb",)] = sc_toks[-1]
        p1_toks = []
        for pj in range(8):
            mb = pj % 2
            mms = []
            for oj in range(8):
                w0 = []
                if oj == 0:
                    w0 = mer_toks + wp_toks + [rel.get(("psM2", mb))]
                mms.append(k.step("tensor", lambda e, _pj=pj, _oj=oj, _mb=mb: e.matmul(
                    psM2[_mb][:], wpT[:, _oj * 1024 + _pj * 128:_oj * 1024 + _pj * 128 + 128],
                    mer[:, _oj * TG:(_oj + 1) * TG],
                    start=(_oj == 0), stop=(_oj == 7)), waits=w0, inc=(oj == 7)))
            t_p1 = k.step("vector", lambda e, _pj=pj, _mb=mb: e.tensor_tensor(
                p1b[:, _pj * TG:(_pj + 1) * TG], psM2[_mb][:],
                g1b[:, _pj * TG:(_pj + 1) * TG], ALU.mult),
                waits=[mms[-1], g1_toks[pj], rel.get(("p1b", pj))])
            rel[("psM2", mb)] = t_p1
            p1_toks.append(t_p1)
        rel[("g1b",)] = p1_toks[-1]
        tr_last = None
        for tt in range(TG // 128):
            ti = (tg * TG) // 128 + tt
            t_x1 = k.dma("sync", o1t[:], out1[ti * 128:(ti + 1) * 128, :],
                         waits=[rel.get(("o1t",))])
            cps = []
            for pj in range(8):
                t_tr = k.step("tensor", lambda e, _pj=pj, _tt=tt: e.transpose(
                    psTr[:], p1b[:, _pj * TG + _tt * 128:_pj * TG + _tt * 128 + 128],
                    idnb[:]),
                    waits=[p1_toks[pj], t_idnb, rel.get(("psTr",))])
                t_c = k.step("vector", lambda e, _pj=pj: e.tensor_copy(
                    p1nat[:, _pj * 128:(_pj + 1) * 128], psTr[:]),
                    waits=[t_tr, rel.get(("p1nat",))])
                rel[("psTr",)] = t_c
                cps.append(t_c)
                tr_last = t_tr
            t_o = k.step("vector", lambda e: e.tensor_add(
                o1t[:], o1t[:], p1nat[:]),
                waits=[t_x1] + cps)
            rel[("p1nat",)] = t_o
            t_so = k.dma("scalar", out[ti * 128:(ti + 1) * 128, :], o1t[:],
                         waits=[t_o])
            rel[("o1t",)] = t_so
        for pj in range(8):
            rel[("p1b", pj)] = tr_last
        for oj in range(8):
            rel[("mer", oj)] = p1_toks[-1]
        for dt in range(16):
            rel[("scano", dt)] = mer_toks[-1]
            rel[("xs", dt)] = [mmdl[-1] if False else sc_toks[-1], mer_toks[-1]]
        rel[("dtlow",)] = sc_toks[-1]
        rel[("Bsb",)] = rep_toks[15]
        rel[("Csb",)] = rep_toks[31]
        rel[("Brep",)] = sc_toks[-1]
        rel[("Crep",)] = sc_toks[-1]


PHASE_BUILDERS = {"p0": build_p0, "p1": build_p1, "p2": build_p2, "p3": build_p3, "p4": build_p4, "p5": build_p5, "p6": build_p6}




def build_phase_program(phase):
    nc = bass.Bass("TRN2", target_bir_lowering=False)
    reads, writes = PHASE_IO[phase]
    io = declare_io(nc, reads, writes)
    k = K(nc)
    with ExitStack() as st:
        k.init_cst_pool(st)
        PHASE_BUILDERS[phase](k, io, st)
        k.barrier()
        k.emit()
    return nc


def declare_io(nc, names_in, names_out):
    io = {}
    specs = _specs()
    for n in names_in:
        io[n] = nc.dram_tensor(n, specs[n][0], specs[n][1], kind="ExternalInput").ap()
    for n in names_out:
        io[n] = nc.dram_tensor(n, specs[n][0], specs[n][1], kind="ExternalOutput").ap()
    return io


def np_dtype(dt):
    return np.float32 if dt == F32 else BF16NP


def run_phases(inputs, upto=None, only=None):
    phases = PHASES if only is None else only
    if upto is not None:
        phases = PHASES[:PHASES.index(upto) + 1]
    consts = build_consts()
    x = np.ascontiguousarray(np.asarray(inputs["L0_attn_x"], np.float32))
    state = {"x": [x[b] for b in range(NCORES)]}
    for wn in WEIGHT_NAMES:
        state[wn] = [np.ascontiguousarray(np.asarray(inputs[wn], np.float32))] * NCORES
    for cn, cv in consts.items():
        state[cn] = [cv] * NCORES
    specs = _specs()
    for ph in phases:
        if ph not in PHASE_BUILDERS:
            raise NotImplementedError(f"phase {ph} not built yet")
        nc = build_phase_program(ph)
        reads, writes = PHASE_IO[ph]
        in_maps = [{n: state[n][b] for n in reads} for b in range(NCORES)]
        res = run_bass_kernel_spmd(nc, in_maps, core_ids=list(range(NCORES)))
        for n in writes:
            state[n] = [np.asarray(res.results[b][n]) for b in range(NCORES)]
    return state


def build_fused():
    nc = bass.Bass("TRN2", target_bir_lowering=False)
    specs = _specs()
    io = {}
    ext_in = set(["x"]) | set(WEIGHT_NAMES) | set(CONST_SPECS)
    for n in sorted(ext_in):
        io[n] = nc.dram_tensor(n, specs[n][0], specs[n][1], kind="ExternalInput").ap()
    io["out"] = nc.dram_tensor("out", specs["out"][0], specs["out"][1],
                               kind="ExternalOutput").ap()
    for n in SCRATCH_SPECS:
        io[n] = nc.dram_tensor(n, specs[n][0], specs[n][1], kind="Internal").ap()
    k = K(nc)
    with ExitStack() as st0:
        k.init_cst_pool(st0)
        for ph in PHASES:
            with ExitStack() as st:
                PHASE_BUILDERS[ph](k, io, st)
            k.barrier()
        k.emit()
    return nc


def kernel(**inputs):
    if not FUSED:
        state = run_phases(inputs)
        return np.stack([state["out"][b] for b in range(NCORES)]).astype(np.float32)
    consts = build_consts()
    x = np.ascontiguousarray(np.asarray(inputs["L0_attn_x"], np.float32))
    base = {}
    for wn in WEIGHT_NAMES:
        base[wn] = np.ascontiguousarray(np.asarray(inputs[wn], np.float32))
    base.update(consts)
    in_maps = [dict(base, x=x[b]) for b in range(NCORES)]
    nc = build_fused()
    res = run_bass_kernel_spmd(nc, in_maps, core_ids=list(range(NCORES)))
    return np.stack([np.asarray(res.results[b]["out"], np.float32)
                     for b in range(NCORES)])
